# version 17
# baseline (speedup 1.0000x reference)
import numpy as np
from contextlib import ExitStack

import concourse.bass as bass
import concourse.mybir as mybir
from concourse.bass_utils import run_bass_kernel_spmd

F32 = mybir.dt.float32
BF16 = mybir.dt.bfloat16
I32 = mybir.dt.int32
AF = mybir.ActivationFunctionType
ALU = mybir.AluOpType
AX = mybir.AxisListType

B, T, D = 4, 4096, 2048
ST = 512
N_ST = T // ST
OWN_ST = 4
NTT = ST // 128
N_OWN_TT = OWN_ST * NTT
KVH, QH, HD = 4, 16, 128
DCH = D // 128
GRID = 16
EPS = 1e-5
BSHIFT = 12.0
KILL = -1e9
SCALE_QK = float(1.0 / np.sqrt(HD))
P_OTHER = [1, 2, 3, 4]
BIGF = 1e30
EPSI = float(np.float32(0.0009765625))
CLIP16 = float(np.float32(np.float32(1.0 - 1e-6) * np.float32(16.0)))
OWN_SLOTS = ([0, 3, 4, 7], [1, 2, 5, 6])

TAPS = []
CC_CORES = 8
PHASE_LIMIT = 6
P1_ST = 8
P1_KV = True
P1_FFN = True
P1_TT = 4


def build_program():
    nc = bass.Bass("TRN2", target_bir_lowering=False, debug=False, detect_race_conditions=False)

    xk = nc.dram_tensor("xk", [T, D], F32, kind="ExternalInput").ap()
    wqd = nc.dram_tensor("wq", [16, D, 128], F32, kind="ExternalInput").ap()
    wkd = nc.dram_tensor("wk", [D, 512], F32, kind="ExternalInput").ap()
    wvd = nc.dram_tensor("wv", [D, 512], F32, kind="ExternalInput").ap()
    wpd = nc.dram_tensor("wp", [16, D, 128], F32, kind="ExternalInput").ap()
    gammad = nc.dram_tensor("gamma", [D], F32, kind="ExternalInput").ap()
    betad = nc.dram_tensor("beta", [D], F32, kind="ExternalInput").ap()
    basesd = nc.dram_tensor("bases", [64, 512], F32, kind="ExternalInput").ap()
    slopesd = nc.dram_tensor("slopes", [64, 512], F32, kind="ExternalInput").ap()
    oscaled = nc.dram_tensor("oscale", [64], F32, kind="ExternalInput").ap()
    coskd = nc.dram_tensor("cosk", [128, T], F32, kind="ExternalInput").ap()
    sinkd = nc.dram_tensor("sink", [128, T], F32, kind="ExternalInput").ap()
    trimd = nc.dram_tensor("trimask", [128, 128], F32, kind="ExternalInput").ap()
    identd = nc.dram_tensor("ident", [128, 128], F32, kind="ExternalInput").ap()
    btabd = nc.dram_tensor("btab", [128, 16], F32, kind="ExternalInput").ap()
    y = nc.dram_tensor("y", [OWN_ST * ST, D], F32, kind="ExternalOutput").ap()

    xnt_spill = nc.dram_tensor("xnt_spill", [OWN_ST, DCH, 128, ST], BF16).ap()
    cc_in_t = nc.dram_tensor("cc_in", [1, 128], F32)
    cc_out_t = nc.dram_tensor("cc_out", [1, 128], F32)
    cc_in = cc_in_t.ap()
    cc_out = cc_out_t.ap()

    taps = {}
    def tap(name, shape, dt=F32):
        if name in TAPS:
            taps[name] = nc.dram_tensor("tap_" + name, shape, dt,
                                        kind="ExternalOutput").ap()
            return taps[name]
        return None

    class Ctr:
        def __init__(self, h):
            self.h = h
            self.n = 0

    pend = {e: [] for e in ("sync", "gpsimd", "vector", "scalar", "tensor")}

    def E(eng, fn, waits=(), inc=None, inc_k=1, rb=None):
        wl = []
        for w in waits:
            if isinstance(w, tuple):
                if w[1] > 0:
                    wl.append((w[0].h, w[1]))
            elif w.n > 0:
                wl.append((w.h, w.n))
        if inc is not None:
            inc.n += inc_k
        pend[eng].append((fn, wl, None if inc is None else inc.h, inc_k, rb))

    def dma(eng, dst, src, ctr, waits=(), transpose=False):
        def f(e, d=dst, s=src, tr=transpose):
            if tr:
                return e.dma_start(out=d, in_=s, transpose=True)
            return e.dma_start(out=d, in_=s)
        E(eng, f, waits=waits, inc=ctr, inc_k=16)

    with ExitStack() as stack:
        def sb(name, shape, dt, st=stack):
            return st.enter_context(nc.sbuf_tensor(name, shape, dt))

        def psb(name, shape, st=stack):
            return st.enter_context(nc.psum_tensor(name, shape, F32))

        def sem(name, st=stack):
            return Ctr(st.enter_context(nc.semaphore(name)))

        block = stack.enter_context(nc.Block())

        def flush():
            for eng in ("sync", "gpsimd", "vector", "scalar", "tensor"):
                lst = pend[eng]
                if not lst:
                    continue
                pend[eng] = []

                def body(e, lst=lst, eng=eng):
                    for fn, wl, sem_h, k, rb in lst:
                        for h, v in wl:
                            e.wait_ge(h, v)
                        ins = fn(e)
                        if eng == "vector" and ins is not None:
                            ins.then_inc(s_vv.h, 1)
                            s_vv.n += 1
                            e.wait_ge(s_vv.h, s_vv.n)
                            if sem_h is not None:
                                e.nop().then_inc(sem_h, k)
                        elif eng == "scalar" and rb is not None:
                            rbi = e.copy(jnk[0:rb.shape[0], 0:1], rb)
                            if sem_h is not None:
                                rbi.then_inc(sem_h, k)
                        else:
                            if sem_h is not None:
                                if ins is None:
                                    ins = e.nop()
                                ins.then_inc(sem_h, k)

                getattr(block, eng)(body)

        KT = sb("KT", [128, KVH * T], BF16)
        V = sb("V", [128, (T // 128) * 512], BF16)
        tsig = sb("tsig", [128, DCH * 64], BF16)
        basesb = sb("basesb", [64, 512], F32)
        slopesb = sb("slopesb", [64, 512], F32)
        oscb = sb("oscb", [128, 64], F32)
        trim = sb("trim", [128, 128], BF16)
        idn = sb("idn", [128, 128], F32)
        identb = sb("identb", [128, 128], BF16)
        btabb = sb("btabb", [128, 16], F32)
        ones1b = sb("ones1b", [128, 1], BF16)
        ones1f = sb("ones1f", [1, 128], F32)
        cbias = sb("cbias", [128, 1], F32)
        negbig = sb("negbig", [128, 1], F32)
        gfull = sb("gfull", [128, 512], BF16)
        oneh = sb("oneh", [128, N_OWN_TT * 64], F32)
        onehi = sb("onehi", [128, 64], I32)
        xsl = sb("xsl", [128, N_OWN_TT * 32], F32)
        delta = sb("delta", [128, N_OWN_TT * 32], F32)
        stats = sb("stats", [128, N_OWN_TT * 8], F32)
        mmacc = sb("mmacc", [128, 1], F32)
        jnk = sb("jnk", [128, 1], F32)
        tminr = sb("tminr", [128, 64], F32)
        tmaxr = sb("tmaxr", [128, 64], F32)
        scrA = sb("scrA", [128, 2048], F32)
        sml = sb("sml", [128, 128], F32)
        sml2 = sb("sml2", [128, 64], F32)
        sstT = sb("sstT", [32, 64], BF16)
        st8 = sb("st8", [128, 8], F32)

        ps = [psb(f"ps{i}", [128, 512]) for i in range(8)]

        s_ld = sem("s_ld")
        s_ldx = sem("s_ldx")
        s_tr = sem("s_tr")
        s_v = sem("s_v")
        s_s = sem("s_s")
        s_t = sem("s_t")
        s_sp = sem("s_sp")
        s_g = sem("s_g")
        s_cc = sem("s_cc")
        s_out = sem("s_out")
        s_xa = sem("s_xa")
        s_xb = sem("s_xb")
        s_vv = sem("s_vv")

        def soft_barrier():
            allsems = [s_ld, s_ldx, s_tr, s_v, s_s, s_t, s_sp, s_g, s_cc,
                       s_out, s_xa, s_xb]
            for eng in ("sync", "gpsimd", "vector", "scalar", "tensor"):
                E(eng, lambda e: None, waits=list(allsems))

        def _finish_early():
            soft_barrier()
            flush()

        dma("gpsimd", basesb[:, :], basesd, s_ld)
        dma("gpsimd", slopesb[:, :], slopesd, s_ld)
        dma("gpsimd", oscb[:, :], oscaled.unsqueeze(0).broadcast_to([128, 64]), s_ld)
        dma("gpsimd", trim[:, :], trimd, s_ld)
        dma("gpsimd", idn[:, :], identd, s_ld)
        dma("gpsimd", btabb[:, :], btabd, s_ld)
        E("gpsimd", lambda e: e.iota(gfull[:, :], pattern=[[0, 32], [1, 16]], base=0,
                                     channel_multiplier=0,
                                     allow_small_or_imprecise_dtypes=True), inc=s_g)
        E("vector", lambda e: e.memset(ones1b[:, :], 1.0))
        E("vector", lambda e: e.memset(ones1f[:, :], 1.0))
        E("vector", lambda e: e.memset(cbias[:, :], -BSHIFT))
        E("vector", lambda e: e.memset(negbig[:, :], -BIGF))
        E("vector", lambda e: e.memset(mmacc[:, :], -BIGF))
        E("vector", lambda e: e.tensor_copy(identb[:, :], idn[:, :]),
          waits=[s_ld])
        E("vector", lambda e: e.memset(tsig[:, :], 0.0), inc=s_v)
        E("vector", lambda e: e.tensor_reduce(
            sml[0:64, 0:32], slopesb[:, :].rearrange("p (j g) -> p j g", g=GRID),
            AX.X, ALU.add), waits=[s_ld], inc=s_v)
        E("scalar", lambda e: e.activation(sml2[0:64, 0:32], sml[0:64, 0:32], AF.Sign),
          waits=[s_v, s_ld], inc=s_s, rb=sml2[0:64, 31:32])
        E("tensor", lambda e: e.transpose(ps[5][0:32, 0:64], sml2[0:64, 0:32],
                                          idn[0:64, 0:64]), waits=[s_s], inc=s_t)
        E("vector", lambda e: e.tensor_copy(sstT[:, :], ps[5][0:32, 0:64]),
          waits=[s_t], inc=s_v)
        for a in range(4):
            def place(e, a=a):
                return e.tensor_copy(
                    tsig[32 * a:32 * a + 32, a:a + 68 * 15 + 1:68],
                    sstT[0:32, a:a + 61:4])
            E("vector", place, waits=[s_v], inc=s_v)

        flush()

        with ExitStack() as p1:
            wkb = sb("wkb", [128, DCH * 512], BF16, p1)
            wvb = sb("wvb", [128, DCH * 512], BF16, p1)
            gam = sb("gam", [128, D], F32, p1)
            bet = sb("bet", [128, D], F32, p1)
            xbuf = sb("xbuf", [128, 2 * D], F32, p1)
            xnf = sb("xnf", [128, D], F32, p1)
            xn2 = sb("xn2", [128, D], F32, p1)
            sgn = sb("sgn", [128, D], BF16, p1)
            sgnT = sb("sgnT", [128, DCH * 128], BF16, p1)
            xnT = sb("xnT", [128, DCH * 512], BF16, p1)
            coss = sb("coss", [128, 512], F32, p1)
            sins = sb("sins", [128, 512], F32, p1)

            dma("gpsimd", wkb[:, :].rearrange("p (c f) -> p c f", c=DCH),
                wkd.rearrange("(c p) f -> p c f", p=128), s_ld)
            dma("gpsimd", wvb[:, :].rearrange("p (c f) -> p c f", c=DCH),
                wvd.rearrange("(c p) f -> p c f", p=128), s_ld)
            dma("gpsimd", gam[:, :], gammad.unsqueeze(0).broadcast_to([128, D]), s_ld)
            dma("gpsimd", bet[:, :], betad.unsqueeze(0).broadcast_to([128, D]), s_ld)

            t_x0raw = tap("x0raw", [128, D])
            t_xn0 = tap("xn0", [128, D])
            t_xn20 = tap("xn20", [128, D])
            t_ts0 = tap("ts0", [128, 64])
            t_oh0 = tap("oh0", [128, 64])
            t_kt = tap("kt", [128, KVH * T])
            t_v = tap("v", [128, (T // 128) * 512])
            t_xsl0 = tap("xsl0", [128, 32])
            t_stats = tap("statsq", [128, N_OWN_TT * 8])
            t_mm = tap("mmacc", [128, 1])

            NG = N_ST * NTT
            pe_wm = {4: (s_v, 0), 5: (s_v, 0),
                     6: (s_v, 0), 7: (s_v, 0)}
            xnf_free_wm = [0]
            xdone = [0] * (NG + 1)
            xfree_v = [0] * (NG + 1)
            xfree_s = [0] * (NG + 1)

            def load_x(g):
                if g >= NG:
                    return
                xb_off = (g % 2) * D
                w = []
                if g >= 2:
                    w = [(s_v, xfree_v[g - 2]), (s_s, xfree_s[g - 2])]
                slot_sem = s_xa if g % 2 == 0 else s_xb
                dma("gpsimd", xbuf[:, xb_off:xb_off + D],
                    xk[g * 128:(g + 1) * 128, :], slot_sem, waits=w)
                xdone[g] = (slot_sem, slot_sem.n)

            load_x(0)
            for st_i in range(min(N_ST, P1_ST)):
                own = st_i < OWN_ST
                cs_off = 0
                dma("gpsimd", coss[:, cs_off:cs_off + 512],
                    coskd[:, st_i * 512:(st_i + 1) * 512], s_ldx, waits=[s_v])
                dma("gpsimd", sins[:, cs_off:cs_off + 512],
                    sinkd[:, st_i * 512:(st_i + 1) * 512], s_ldx, waits=[s_v])

                for tt in range(min(NTT, P1_TT)):
                    g = st_i * NTT + tt
                    ti = g
                    xb_off = (g % 2) * D
                    xv = xbuf[:, xb_off:xb_off + D]
                    if t_x0raw is not None and g == 0:
                        E("vector", lambda e, xv=xv: e.tensor_copy(scrA[:, :], xv),
                          waits=[xdone[g]], inc=s_v)
                        dma("gpsimd", t_x0raw, scrA[:, :], s_sp, waits=[s_v])
                    E("vector", lambda e, xv=xv: e.reduce_sum(
                        sml[:, 0:64], xv.rearrange("p (a b) -> p a b", a=64), AX.X),
                      waits=[xdone[g], s_s])
                    E("vector", lambda e: e.reduce_sum(st8[:, 0:1], sml[:, 0:64], AX.X))
                    E("vector", lambda e: e.tensor_scalar_mul(st8[:, 2:3], st8[:, 0:1],
                                                              1.0 / D))
                    E("vector", lambda e, xv=xv: e.tensor_single_scalar(
                        xnf[:, :], xv, st8[:, 2:3], ALU.subtract),
                      waits=[(s_t, xnf_free_wm[0])])
                    E("vector", lambda e: e.tensor_tensor(
                        xn2[:, :], xnf[:, :], xnf[:, :], ALU.mult))
                    E("vector", lambda e: e.reduce_sum(
                        sml[:, 0:64], xn2[:, :].rearrange("p (a b) -> p a b", a=64),
                        AX.X))
                    E("vector", lambda e: e.reduce_sum(st8[:, 1:2], sml[:, 0:64], AX.X))
                    xfree_s[g] = s_s.n
                    E("vector", lambda e: e.tensor_scalar(st8[:, 3:4], st8[:, 1:2],
                                                          1.0 / D, EPS, ALU.mult,
                                                          ALU.add), inc=s_v)
                    E("scalar", lambda e: e.activation(st8[:, 5:6], st8[:, 3:4], AF.Sqrt),
                      waits=[s_v], inc=s_s, rb=st8[:, 5:6])
                    E("vector", lambda e: e.reciprocal(st8[:, 6:7], st8[:, 5:6]),
                      waits=[s_s])
                    E("vector", lambda e: e.tensor_tensor(st8[:, 7:8], st8[:, 6:7],
                                                          st8[:, 6:7], ALU.mult))
                    E("vector", lambda e: e.tensor_tensor(st8[:, 7:8], st8[:, 3:4],
                                                          st8[:, 7:8], ALU.mult))
                    E("vector", lambda e: e.tensor_scalar(st8[:, 7:8], st8[:, 7:8],
                                                          -0.5, 1.5, ALU.mult, ALU.add))
                    E("vector", lambda e: e.tensor_tensor(st8[:, 4:5], st8[:, 6:7],
                                                          st8[:, 7:8], ALU.mult))
                    E("vector", lambda e: e.tensor_scalar_mul(
                        xnf[:, :], xnf[:, :], st8[:, 4:5]), inc=s_v)
                    xfree_v[g] = s_v.n
                    load_x(g + 1)
                    if own:
                        E("vector", lambda e, ti=ti: e.tensor_copy(
                            stats[:, ti * 8:ti * 8 + 1], st8[:, 2:3]))
                        E("vector", lambda e, ti=ti: e.tensor_copy(
                            stats[:, ti * 8 + 1:ti * 8 + 2], st8[:, 4:5]))
                        E("vector", lambda e, ti=ti: e.tensor_tensor(
                            stats[:, ti * 8 + 7:ti * 8 + 8], st8[:, 2:3], st8[:, 4:5],
                            ALU.mult))
                        E("vector", lambda e, ti=ti: e.tensor_scalar_mul(
                            stats[:, ti * 8 + 7:ti * 8 + 8],
                            stats[:, ti * 8 + 7:ti * 8 + 8], -1.0), inc=s_v)
                    if t_xn0 is not None and g == 0:
                        E("vector", lambda e: e.tensor_copy(scrA[:, :], xnf[:, :]),
                          waits=[s_sp], inc=s_v)
                        dma("gpsimd", t_xn0, scrA[:, :], s_sp, waits=[s_v])
                    for q in range(4):
                        bk = (6 + (q % 2)) if own else (4 + q)
                        for j in range(4):
                            c = q * 4 + j
                            dw = [(s_v, xfree_v[g])] if q == 0 and j == 0 else []
                            E("tensor", lambda e, c=c, j=j, bk=bk: e.transpose(
                                ps[bk][:, j * 128:(j + 1) * 128],
                                xnf[:, c * 128:(c + 1) * 128], idn[:, :]),
                                waits=dw + ([pe_wm[bk]] if j == 0 else []),
                                inc=s_t)
                        twm = s_t.n
                        ew = [(s_t, twm)]
                        if q == 0 and tt == 0:
                            ew.append(s_sp)
                        E("vector", lambda e, q=q, tt=tt, bk=bk: e.tensor_copy(
                            xnT[:, :].rearrange("p (c w) -> p c w", w=512)[
                                :, q * 4:(q + 1) * 4, tt * 128:(tt + 1) * 128],
                            ps[bk][:, :].rearrange("p (j w) -> p j w", w=128)),
                            waits=ew, inc=s_v)
                        pe_wm[bk] = (s_v, s_v.n)
                    xnf_free_wm[0] = s_t.n
                    last_xnT_ev = s_v.n

                    if own and P1_FFN:
                        E("vector", lambda e: e.reduce_sum(
                            sml[:, 0:64], xnf[:, :].rearrange("p (a b) -> p a b", a=64),
                            AX.X))
                        E("vector", lambda e: e.reduce_sum(st8[:, 0:1], sml[:, 0:64],
                                                           AX.X))
                        E("vector", lambda e: e.tensor_scalar_mul(st8[:, 2:3],
                                                                  st8[:, 0:1], 1.0 / D))
                        E("vector", lambda e: e.tensor_single_scalar(
                            scrA[:, :], xnf[:, :], st8[:, 2:3], ALU.subtract))
                        E("vector", lambda e: e.tensor_tensor(
                            xn2[:, :], scrA[:, :], scrA[:, :], ALU.mult))
                        E("vector", lambda e: e.reduce_sum(
                            sml[:, 0:64], xn2[:, :].rearrange("p (a b) -> p a b", a=64),
                            AX.X))
                        E("vector", lambda e: e.reduce_sum(st8[:, 1:2], sml[:, 0:64],
                                                           AX.X))
                        E("vector", lambda e: e.tensor_scalar(st8[:, 3:4], st8[:, 1:2],
                                                              1.0 / D, EPS, ALU.mult,
                                                              ALU.add), inc=s_v)
                        E("scalar", lambda e: e.activation(st8[:, 5:6], st8[:, 3:4],
                                                           AF.Sqrt),
                          waits=[s_v], inc=s_s, rb=st8[:, 5:6])
                        E("vector", lambda e: e.reciprocal(st8[:, 6:7], st8[:, 5:6]),
                          waits=[s_s])
                        E("vector", lambda e: e.tensor_tensor(st8[:, 7:8], st8[:, 6:7],
                                                              st8[:, 6:7], ALU.mult))
                        E("vector", lambda e: e.tensor_tensor(st8[:, 7:8], st8[:, 3:4],
                                                              st8[:, 7:8], ALU.mult))
                        E("vector", lambda e: e.tensor_scalar(st8[:, 7:8], st8[:, 7:8],
                                                              -0.5, 1.5, ALU.mult,
                                                              ALU.add))
                        E("vector", lambda e: e.tensor_tensor(st8[:, 4:5], st8[:, 6:7],
                                                              st8[:, 7:8], ALU.mult))
                        E("vector", lambda e: e.scalar_tensor_tensor(
                            xn2[:, :], scrA[:, :], st8[:, 4:5], gam[:, :], ALU.mult,
                            ALU.mult), waits=[s_ld])
                        E("vector", lambda e: e.tensor_add(xn2[:, :], xn2[:, :], bet[:, :]),
                          inc=s_v)
                        if t_xn20 is not None and g == 0:
                            dma("gpsimd", t_xn20, xn2[:, :], s_sp, waits=[s_v])
                        E("scalar", lambda e: e.activation(scrA[:, :], xn2[:, :], AF.Sign),
                          waits=[s_v], inc=s_s, rb=scrA[:, D - 1:D])
                        sign_wm = s_s.n
                        for q in range(4):
                            bk = 6 + (q % 2)
                            for j in range(4):
                                c = q * 4 + j
                                dw = [(s_s, sign_wm)] if q == 0 and j == 0 else []
                                E("tensor", lambda e, c=c, j=j, bk=bk: e.transpose(
                                    ps[bk][:, j * 128:(j + 1) * 128],
                                    scrA[:, c * 128:(c + 1) * 128], idn[:, :]),
                                    waits=dw + ([pe_wm[bk]] if j == 0 else []),
                                    inc=s_t)
                            twm = s_t.n
                            E("vector", lambda e, q=q, bk=bk: e.tensor_copy(
                                sgnT[:, q * 512:(q + 1) * 512], ps[bk][:, :]),
                                waits=[(s_t, twm)], inc=s_v)
                            pe_wm[bk] = (s_v, s_v.n)
                        for c in range(DCH):
                            E("tensor", lambda e, c=c: e.matmul(
                                ps[4][:, 0:64], sgnT[:, c * 128:(c + 1) * 128],
                                tsig[:, c * 64:(c + 1) * 64],
                                start=(c == 0), stop=(c == DCH - 1)),
                                waits=[s_s, s_v] if c == 0 else [],
                                inc=s_t if c == DCH - 1 else None)
                        E("vector", lambda e: e.tensor_copy(scrA[:, 1984:2048], ps[4][:, 0:64]),
                          waits=[s_t], inc=s_v)
                        if t_ts0 is not None and g == 0:
                            dma("gpsimd", t_ts0, scrA[:, 1984:2048], s_sp, waits=[s_v])
                        E("vector", lambda e: e.tensor_reduce(
                            sml[:, 0:8], scrA[:, 1984:2048].rearrange("p (c t) -> p c t", t=8),
                            AX.X, ALU.add))
                        E("vector", lambda e: e.scalar_tensor_tensor(
                            sml[:, 8:16], gfull[:, 0:8], -EPSI, sml[:, 0:8],
                            ALU.mult, ALU.add), waits=[s_g])
                        E("vector", lambda e: e.tensor_reduce(
                            sml[:, 16:17], sml[:, 8:16], AX.X, ALU.max))
                        E("vector", lambda e: e.tensor_tensor(
                            sml[:, 24:32], sml[:, 8:16],
                            sml[:, 16:17].broadcast_to([128, 8]), ALU.is_ge))
                        E("vector", lambda e: e.scalar_tensor_tensor(
                            sml2[:, 0:64], gfull[:, 0:64], -EPSI, scrA[:, 1984:2048],
                            ALU.mult, ALU.add))
                        E("vector", lambda e: e.scalar_tensor_tensor(
                            sml2[:, 0:64].rearrange("p (c t) -> p c t", t=8),
                            sml[:, 24:32].unsqueeze(2).broadcast_to([128, 8, 8]),
                            512.0,
                            sml2[:, 0:64].rearrange("p (c t) -> p c t", t=8),
                            ALU.mult, ALU.add))
                        E("vector", lambda e: e.tensor_reduce(
                            sml[:, 32:33], sml2[:, 0:64], AX.X, ALU.max))
                        E("vector", lambda e, ti=ti: e.tensor_tensor(
                            oneh[:, ti * 64:(ti + 1) * 64], sml2[:, 0:64],
                            sml[:, 32:33].broadcast_to([128, 64]), ALU.is_ge))
                        E("vector", lambda e, ti=ti: e.tensor_copy(
                            onehi[:, :], oneh[:, ti * 64:(ti + 1) * 64]), inc=s_v)
                        if t_oh0 is not None and g == 0:
                            dma("gpsimd", t_oh0, oneh[:, 0:64], s_sp, waits=[s_v])
                        E("vector", lambda e: e.tensor_reduce(
                            sml[:, 0:64], xn2[:, :].rearrange("p (t j) -> p t j", j=32),
                            AX.X, ALU.min))
                        E("vector", lambda e: e.tensor_reduce(
                            sml[:, 64:128], xn2[:, :].rearrange("p (t j) -> p t j", j=32),
                            AX.X, ALU.max))
                        for (src_off, dst_slot) in ((0, 2), (64, 3)):
                            E("vector", lambda e, ti=ti, so=src_off: e.tensor_tensor(
                                sml2[:, 0:64], oneh[:, ti * 64:(ti + 1) * 64],
                                sml[:, so:so + 64], ALU.mult))
                            E("vector", lambda e, ti=ti, sl=dst_slot: e.tensor_reduce(
                                stats[:, ti * 8 + sl:ti * 8 + sl + 1], sml2[:, 0:64],
                                AX.X, ALU.add))
                        E("vector", lambda e, ti=ti: e.tensor_tensor(
                            sml2[:, 0:64], oneh[:, ti * 64:(ti + 1) * 64], oscb[:, :],
                            ALU.mult))
                        E("vector", lambda e, ti=ti: e.tensor_reduce(
                            stats[:, ti * 8 + 4:ti * 8 + 5], sml2[:, 0:64], AX.X, ALU.add))
                        E("vector", lambda e, ti=ti: e.tensor_tensor(
                            scrA[:, :].rearrange("p (t j) -> p t j", j=32),
                            xn2[:, :].rearrange("p (t j) -> p t j", j=32),
                            oneh[:, ti * 64:(ti + 1) * 64].unsqueeze(2).broadcast_to(
                                [128, 64, 32]), ALU.mult))
                        E("vector", lambda e, ti=ti: e.tensor_reduce(
                            xsl[:, ti * 32:(ti + 1) * 32],
                            scrA[:, :].rearrange("p (t j) -> p j t", t=64),
                            AX.X, ALU.add), inc=s_v)
                        if t_xsl0 is not None and g == 0:
                            dma("gpsimd", t_xsl0, xsl[:, 0:32], s_sp, waits=[s_v])
                        E("vector", lambda e, ti=ti: e.tensor_scalar_mul(
                            st8[:, 6:7], stats[:, ti * 8 + 2:ti * 8 + 3], -1.0))
                        E("vector", lambda e: e.select(
                            sml[:, 0:64], onehi[:, :],
                            st8[:, 6:7].broadcast_to([128, 64]),
                            negbig[:, :].broadcast_to([128, 64])))
                        E("vector", lambda e, ti=ti: e.select(
                            sml[:, 64:128], onehi[:, :],
                            stats[:, ti * 8 + 3:ti * 8 + 4].broadcast_to([128, 64]),
                            negbig[:, :].broadcast_to([128, 64])), inc=s_v)
                        E("tensor", lambda e: e.transpose(ps[5][:, 0:128], sml[:, :],
                                                          idn[:, :]),
                          waits=[s_v], inc=s_t)
                        E("vector", lambda e: e.tensor_reduce(
                            sml2[:, 0:1], ps[5][:, 0:128], AX.X, ALU.max), waits=[s_t])
                        E("vector", lambda e: e.tensor_max(
                            mmacc[:, :], mmacc[:, :], sml2[:, 0:1]), inc=s_v)

                for h4 in range(KVH if P1_KV else 0):
                    bank = h4 % 2
                    for c in range(DCH):
                        E("tensor", lambda e, h4=h4, c=c, bank=bank: e.matmul(
                            ps[bank][:, :],
                            wkb[:, c * 512 + h4 * 128: c * 512 + (h4 + 1) * 128],
                            xnT[:, c * 512:(c + 1) * 512],
                            start=(c == 0), stop=(c == DCH - 1)),
                            waits=[s_ld, s_v] if c == 0 else [],
                            inc=s_t if c == DCH - 1 else None)
                    kt_off = h4 * T + st_i * 512
                    E("vector", lambda e, bank=bank, co=cs_off: e.tensor_tensor(
                        scrA[:, 0:512], ps[bank][:, :], coss[:, co:co + 512],
                        ALU.mult), waits=[s_t, s_ldx])
                    E("vector", lambda e, bank=bank, co=cs_off: e.tensor_tensor(
                        scrA[0:64, 512:1024], ps[bank][64:128, :],
                        sins[0:64, co:co + 512], ALU.mult))
                    E("vector", lambda e, bank=bank, co=cs_off: e.tensor_tensor(
                        scrA[64:128, 512:1024], ps[bank][0:64, :],
                        sins[64:128, co:co + 512], ALU.mult))
                    E("vector", lambda e, kt_off=kt_off: e.tensor_add(
                        KT[:, kt_off:kt_off + 512], scrA[:, 0:512],
                        scrA[:, 512:1024]), inc=s_v)
                for ksub in range(NTT if P1_KV else 0):
                    bank = 2 + ksub % 2
                    for c in range(DCH):
                        E("tensor", lambda e, ksub=ksub, c=c, bank=bank: e.matmul(
                            ps[bank][:, :],
                            xnT[:, c * 512 + ksub * 128: c * 512 + (ksub + 1) * 128],
                            wvb[:, c * 512:(c + 1) * 512],
                            start=(c == 0), stop=(c == DCH - 1)),
                            waits=[s_ld, s_v] if c == 0 else [],
                            inc=s_t if c == DCH - 1 else None)
                    v_off = (st_i * NTT + ksub) * 512
                    E("vector", lambda e, bank=bank, v_off=v_off: e.tensor_copy(
                        V[:, v_off:v_off + 512], ps[bank][:, :]), waits=[s_t], inc=s_v)

                if own:
                    dma("gpsimd", xnt_spill[st_i].rearrange("c p f -> p c f"),
                        xnT[:, :].rearrange("p (c f) -> p c f", c=DCH), s_sp,
                        waits=[(s_v, last_xnT_ev), s_t])
                flush()

            if t_kt is not None:
                dma("gpsimd", t_kt, KT[:, :], s_sp, waits=[s_v])
            if t_v is not None:
                dma("gpsimd", t_v, V[:, :], s_sp, waits=[s_v])
            if t_stats is not None:
                dma("gpsimd", t_stats, stats[:, :], s_sp, waits=[s_v])
            if t_mm is not None:
                dma("gpsimd", t_mm, mmacc[:, :], s_sp, waits=[s_v])
            flush()

        if PHASE_LIMIT <= 1:
            _finish_early()
            return nc, taps
        soft_barrier()
        dma("gpsimd", cc_in.rearrange("o f -> f o"), mmacc[:, :], s_sp, waits=[s_v])
        E("gpsimd", lambda e: e.collective_compute(
            "AllReduce", ALU.max, replica_groups=[list(range(CC_CORES))],
            ins=[cc_in.opt()], outs=[cc_out.opt()]),
            waits=[s_sp], inc=s_cc)
        dma("gpsimd", tminr[:, :], cc_out[0:1, 0:64].broadcast_to([128, 64]),
            s_ld, waits=[s_cc])
        dma("gpsimd", tmaxr[:, :], cc_out[0:1, 64:128].broadcast_to([128, 64]),
            s_ld, waits=[s_cc])
        E("vector", lambda e: e.tensor_scalar_mul(tminr[:, :], tminr[:, :], -1.0),
          waits=[s_ld], inc=s_v)
        t_tmin = tap("tmin", [128, 64])
        if t_tmin is not None:
            dma("gpsimd", t_tmin, tminr[:, :], s_sp, waits=[s_v])
        flush()

        if PHASE_LIMIT <= 2:
            _finish_early()
            return nc, taps
        soft_barrier()
        with ExitStack() as pc:
            onehT = sb("onehT", [64, N_OWN_TT * 128], F32, pc)
            scrB = sb("scrB", [128, 1024], F32, pc)
            t_d0 = tap("delta0", [128, 32])
            t_u0 = tap("u0", [128, 32])

            for ti in range(N_OWN_TT):
                ohv = oneh[:, ti * 64:(ti + 1) * 64]
                E("vector", lambda e, ohv=ohv: e.tensor_tensor(
                    scrB[:, 0:64], ohv, tminr[:, :], ALU.mult), waits=[s_v])
                E("vector", lambda e, ti=ti: e.tensor_reduce(
                    stats[:, ti * 8 + 5:ti * 8 + 6], scrB[:, 0:64], AX.X, ALU.add))
                E("vector", lambda e, ohv=ohv: e.tensor_tensor(
                    scrB[:, 0:64], ohv, tmaxr[:, :], ALU.mult))
                E("vector", lambda e: e.tensor_reduce(
                    st8[:, 0:1], scrB[:, 0:64], AX.X, ALU.add))
                E("vector", lambda e, ti=ti: e.tensor_sub(
                    st8[:, 1:2], st8[:, 0:1], stats[:, ti * 8 + 5:ti * 8 + 6]))
                E("vector", lambda e: e.tensor_scalar_add(st8[:, 1:2], st8[:, 1:2], 1e-8))
                E("vector", lambda e: e.reciprocal(st8[:, 2:3], st8[:, 1:2]))
                E("vector", lambda e: e.tensor_tensor(st8[:, 3:4], st8[:, 1:2],
                                                      st8[:, 2:3], ALU.mult))
                E("vector", lambda e: e.tensor_scalar(st8[:, 3:4], st8[:, 3:4],
                                                      -1.0, 2.0, ALU.mult, ALU.add))
                E("vector", lambda e, ti=ti: e.tensor_tensor(
                    stats[:, ti * 8 + 6:ti * 8 + 7], st8[:, 2:3], st8[:, 3:4],
                    ALU.mult), inc=s_v)
                E("vector", lambda e, ti=ti: e.tensor_scalar(
                    scrB[:, 0:32], xsl[:, ti * 32:(ti + 1) * 32],
                    stats[:, ti * 8 + 5:ti * 8 + 6], stats[:, ti * 8 + 6:ti * 8 + 7],
                    ALU.subtract, ALU.mult))
                E("vector", lambda e: e.tensor_scalar(
                    scrB[:, 0:32], scrB[:, 0:32], 16.0, CLIP16, ALU.mult, ALU.min))
                E("vector", lambda e: e.tensor_scalar_max(
                    scrB[:, 0:32], scrB[:, 0:32], 0.0), inc=s_v)
                if t_u0 is not None and ti == 0:
                    dma("gpsimd", t_u0, scrB[:, 0:32], s_sp, waits=[s_v])
                E("vector", lambda e: e.tensor_tensor(
                    scrB[:, 512:992].rearrange("p (j g) -> p j g", g=15),
                    scrB[:, 0:32].unsqueeze(2).broadcast_to([128, 32, 15]),
                    gfull[:, :].rearrange("p (j g) -> p j g", g=16)[:, :, 1:16],
                    ALU.is_ge), waits=[s_g])
                E("vector", lambda e: e.tensor_reduce(
                    scrB[:, 32:64], scrB[:, 512:992].rearrange("p (j g) -> p j g", g=15),
                    AX.X, ALU.add))
                E("vector", lambda e: e.tensor_sub(
                    scrB[:, 64:96], scrB[:, 0:32], scrB[:, 32:64]), inc=s_v)
                E("tensor", lambda e, ohv=ohv: e.transpose(
                    ps[5][0:64, 0:128], ohv, idn[:, :]), waits=[s_v], inc=s_t)
                E("vector", lambda e, ti=ti: e.tensor_copy(
                    onehT[:, ti * 128:(ti + 1) * 128], ps[5][0:64, 0:128]),
                    waits=[s_t], inc=s_v)
                E("tensor", lambda e, ti=ti: e.matmul(
                    ps[0][:, :], onehT[:, ti * 128:(ti + 1) * 128], basesb[:, :],
                    start=True, stop=True), waits=[s_v])
                E("tensor", lambda e, ti=ti: e.matmul(
                    ps[1][:, :], onehT[:, ti * 128:(ti + 1) * 128], slopesb[:, :],
                    start=True, stop=True), inc=s_t)
                E("vector", lambda e: e.tensor_tensor(
                    scrA[:, 0:512].rearrange("p (j g) -> p j g", g=16),
                    scrB[:, 32:64].unsqueeze(2).broadcast_to([128, 32, 16]),
                    gfull[:, :].rearrange("p (j g) -> p j g", g=16), ALU.is_equal),
                    waits=[s_t])
                E("vector", lambda e: e.tensor_tensor(
                    scrA[:, 512:1024], scrA[:, 0:512], ps[0][:, :], ALU.mult))
                E("vector", lambda e: e.tensor_reduce(
                    scrB[:, 96:128], scrA[:, 512:1024].rearrange("p (j g) -> p j g", g=16),
                    AX.X, ALU.add))
                E("vector", lambda e: e.tensor_tensor(
                    scrA[:, 512:1024], scrA[:, 0:512], ps[1][:, :], ALU.mult))
                E("vector", lambda e: e.tensor_reduce(
                    scrB[:, 128:160],
                    scrA[:, 512:1024].rearrange("p (j g) -> p j g", g=16),
                    AX.X, ALU.add), inc=s_v)
                E("vector", lambda e: e.tensor_tensor(
                    scrB[:, 160:192], scrB[:, 128:160], scrB[:, 64:96], ALU.mult))
                E("vector", lambda e: e.tensor_add(
                    scrB[:, 160:192], scrB[:, 160:192], scrB[:, 96:128]))
                E("vector", lambda e, ti=ti: e.tensor_scalar_mul(
                    delta[:, ti * 32:(ti + 1) * 32], scrB[:, 160:192],
                    stats[:, ti * 8 + 4:ti * 8 + 5]), inc=s_v)
                if t_d0 is not None and ti == 0:
                    dma("gpsimd", t_d0, delta[:, 0:32], s_sp, waits=[s_v])
            flush()

        t_qt = tap("qt", [128, QH * 2 * 512], BF16)
        t_ot = tap("ot", [128, QH * 2 * 512], BF16)
        t_att0 = tap("att0", [128, D])

        if PHASE_LIMIT <= 3:
            _finish_early()
            return nc, taps
        for g2 in range(2):
            soft_barrier()
            flush()
            with ExitStack() as pg:
                QO = sb(f"QO{g2}", [128, QH * 2 * 512], BF16, pg)
                wqb = sb(f"wqb{g2}", [128, DCH * 128], BF16, pg)
                wqb2 = sb(f"wqb2{g2}", [128, DCH * 128], BF16, pg)
                PT = sb(f"PT{g2}", [128, 4 * 512], BF16, pg)
                recb = sb(f"recb{g2}", [128, 512], F32, pg)
                denr = sb(f"denr{g2}", [1, 512], F32, pg)
                recr = sb(f"recr{g2}", [1, 512], F32, pg)

                with ExitStack() as pd:
                    xnTd = sb(f"xnTd{g2}", [128, 2 * DCH * 512], BF16, pd)
                    cosq = sb(f"cosq{g2}", [128, 2 * 512], F32, pd)
                    sinq = sb(f"sinq{g2}", [128, 2 * 512], F32, pd)
                    for il in range(2):
                        i = 2 * g2 + il
                        dma("gpsimd", xnTd[:, il * DCH * 512:(il + 1) * DCH * 512]
                            .rearrange("p (c f) -> p c f", c=DCH),
                            xnt_spill[i].rearrange("c p f -> p c f"), s_ld,
                            waits=[s_t, s_v])
                        dma("gpsimd", cosq[:, il * 512:(il + 1) * 512],
                            coskd[:, i * 512:(i + 1) * 512], s_ldx, waits=[s_v])
                        dma("gpsimd", sinq[:, il * 512:(il + 1) * 512],
                            sinkd[:, i * 512:(i + 1) * 512], s_ldx, waits=[s_v])
                    d_wm = {}
                    for h in range(QH):
                        wbuf = wqb if h % 2 == 0 else wqb2
                        dma("gpsimd", wbuf[:, :].rearrange("p (c f) -> p c f", c=DCH),
                            wqd[h].rearrange("(c p) f -> p c f", p=128), s_ld,
                            waits=[(s_t, d_wm[h - 2])] if h >= 2 else [s_t])
                        for il in range(2):
                            bank = il
                            for c in range(DCH):
                                E("tensor", lambda e, c=c, bank=bank, wbuf=wbuf, il=il: e.matmul(
                                    ps[bank][:, :], wbuf[:, c * 128:(c + 1) * 128],
                                    xnTd[:, il * DCH * 512 + c * 512:
                                         il * DCH * 512 + (c + 1) * 512],
                                    start=(c == 0), stop=(c == DCH - 1)),
                                    waits=[s_ld, s_v] if c == 0 else [],
                                    inc=s_t if c == DCH - 1 else None)
                            qo_off = (h * 2 + il) * 512
                            co = il * 512
                            E("vector", lambda e, bank=bank, co=co: e.tensor_tensor(
                                scrA[:, 0:512], ps[bank][:, :], cosq[:, co:co + 512],
                                ALU.mult), waits=[s_t, s_ldx])
                            E("vector", lambda e, bank=bank, co=co: e.tensor_tensor(
                                scrA[0:64, 512:1024], ps[bank][64:128, :],
                                sinq[0:64, co:co + 512], ALU.mult))
                            E("vector", lambda e, bank=bank, co=co: e.tensor_tensor(
                                scrA[64:128, 512:1024], ps[bank][0:64, :],
                                sinq[64:128, co:co + 512], ALU.mult))
                            E("vector", lambda e, qo_off=qo_off: e.tensor_add(
                                QO[:, qo_off:qo_off + 512], scrA[:, 0:512],
                                scrA[:, 512:1024]), inc=s_v)
                        d_wm[h] = s_t.n
                    flush()
                if t_qt is not None and g2 == 0:
                    dma("gpsimd", t_qt, QO[:, :], s_sp, waits=[s_v])
                    flush()

                if PHASE_LIMIT <= 4:
                    continue
                for il in range(2):
                    i = 2 * g2 + il
                    blocks = []
                    for m in range(P_OTHER[i]):
                        for b4 in range(4):
                            blocks.append(("oth", m, (16 + 4 * m + b4) * 128))
                    for j in range(4 * i):
                        blocks.append(("own", 0, j * 128))
                    for a in range(4):
                        blocks.append(("diag", a, (4 * i + a) * 128))
                    nb = len(blocks)
                    for h in range(QH):
                        ghead = h // 4
                        qo_off = (h * 2 + il) * 512
                        exp_wm = {}
                        qk_wm = {}
                        STB = [0, 1, 5, 6]

                        def emit_qk(bi):
                            kind, aux, kcol = blocks[bi]
                            bank = STB[bi % 4]
                            lo = 128 * aux if kind == "diag" else 0
                            if bi == 0:
                                w = [s_s, s_v]
                            elif bi >= 4:
                                w = [(s_s, exp_wm[bi - 4])]
                            else:
                                w = []
                            E("tensor", lambda e, bank=bank, kcol=kcol, lo=lo,
                              gh=ghead, qo=qo_off: e.matmul(
                                ps[bank][:, lo:512],
                                KT[:, gh * T + kcol:gh * T + kcol + 128],
                                QO[:, qo + lo:qo + 512], start=True,
                                stop=True), waits=w, inc=s_t)
                            qk_wm[bi] = s_t.n

                        def emit_exp(bi):
                            kind, aux, kcol = blocks[bi]
                            bank = STB[bi % 4]
                            ptile = bi % 4
                            lo = 128 * aux if kind == "diag" else 0
                            w = [(s_t, qk_wm[bi])]
                            if kind == "diag":
                                E("vector", lambda e, bank=bank, lo=lo: e.tensor_tensor(
                                    ps[bank][:, lo:lo + 128], ps[bank][:, lo:lo + 128],
                                    trim[:, :], ALU.add), waits=[(s_t, qk_wm[bi])],
                                    inc=s_v)
                                w.append((s_v, s_v.n))
                            bias = (btabb[:, (4 * i + aux):(4 * i + aux) + 1]
                                    if kind == "oth" else cbias[:, :])
                            E("scalar", lambda e, bank=bank, lo=lo, pt=ptile,
                              bias=bias: e.activation(
                                PT[:, pt * 512 + lo:pt * 512 + 512],
                                ps[bank][:, lo:512], AF.Exp, bias=bias,
                                scale=SCALE_QK), waits=w, inc=s_s,
                              rb=PT[:, ptile * 512 + 511:ptile * 512 + 512])
                            exp_wm[bi] = s_s.n

                        def emit_av(bi):
                            kind, aux, kcol = blocks[bi]
                            ptile = bi % 4
                            lo = 128 * aux if kind == "diag" else 0
                            ktile = kcol // 128
                            last = bi == nb - 1
                            E("tensor", lambda e, kt=ktile, pt=ptile, lo=lo,
                              gh=ghead, first=(bi == 0), last=last: e.matmul(
                                ps[2][:, lo:512],
                                V[:, kt * 512 + gh * 128:kt * 512 + (gh + 1) * 128],
                                PT[:, pt * 512 + lo:pt * 512 + 512],
                                start=first, stop=last),
                                waits=[(s_s, exp_wm[bi])])
                            E("tensor", lambda e, pt=ptile, lo=lo,
                              first=(bi == 0), last=last: e.matmul(
                                ps[3][0:1, lo:512], ones1b[:, :],
                                PT[:, pt * 512 + lo:pt * 512 + 512],
                                start=first, stop=last),
                                inc=s_t if last else None)

                        for bi in range(nb + 2):
                            if bi < nb:
                                emit_qk(bi)
                                emit_exp(bi)
                            if bi >= 2:
                                emit_av(bi - 2)
                        E("scalar", lambda e: e.copy(denr[:, :], ps[3][0:1, :]),
                          waits=[s_t], inc=s_s, rb=denr[0:1, 511:512])
                        E("tensor", lambda e: e.matmul(
                            ps[4][:, :], ones1f[:, :], denr[:, :], start=True,
                            stop=True), waits=[s_s, s_v], inc=s_t)
                        E("vector", lambda e: e.reciprocal(recb[:, :], ps[4][:, :]),
                          waits=[s_t], inc=s_v)
                        E("vector", lambda e, qo=qo_off: e.tensor_tensor(
                            QO[:, qo:qo + 512], ps[2][:, :], recb[:, :], ALU.mult),
                            inc=s_v)
                    flush()
                if t_ot is not None and g2 == 0:
                    dma("gpsimd", t_ot, QO[:, :], s_sp, waits=[s_v])
                    flush()

                if PHASE_LIMIT <= 5:
                    continue
                soft_barrier()
                with ExitStack() as pf:
                    scrB = sb(f"scrBf{g2}", [128, 1024], F32, pf)
                    outb4 = sb(f"outb4{g2}", [128, 4 * D], F32, pf)
                    dltw = sb(f"dltw{g2}", [128, D], F32, pf)
                    xrb = sb(f"xrb{g2}", [128, 2 * D], F32, pf)
                    f_wm = {}
                    fx_slot_wm = [0, 0]
                    for il in range(2):
                        i = 2 * g2 + il
                        for o in range(16):
                            wbuf = wqb if o % 2 == 0 else wqb2
                            w = [(s_t, f_wm[o - 2])] if o >= 2 else (
                                [(s_t, f_wm[14 + o])] if il == 1 else [s_t])
                            dma("gpsimd", wbuf[:, :].rearrange("p (c f) -> p c f", c=DCH),
                                wpd[o].rearrange("(c p) f -> p c f", p=128), s_ld,
                                waits=w)
                            bank = o % 2
                            for h in range(QH):
                                E("tensor", lambda e, h=h, o=o, il=il, bank=bank,
                                  wbuf=wbuf: e.matmul(
                                    ps[bank][:, :], wbuf[:, h * 128:(h + 1) * 128],
                                    QO[:, (h * 2 + il) * 512:(h * 2 + il + 1) * 512],
                                    start=(h == 0), stop=(h == QH - 1)),
                                    waits=[s_ld, s_s, s_v] if h == 0 else [],
                                    inc=s_t if h == QH - 1 else None)
                            f_wm[o] = s_t.n
                            E("scalar", lambda e, bank=bank: e.copy(
                                scrB[:, 0:512], ps[bank][:, :]), waits=[s_t], inc=s_s,
                                rb=scrB[:, 511:512])
                            for sub in range(NTT):
                                pb = 2 + sub % 2
                                E("tensor", lambda e, sub=sub, pb=pb: e.transpose(
                                    ps[pb][:, 0:128], scrB[:, sub * 128:(sub + 1) * 128],
                                    idn[:, :]), waits=[s_s, s_v], inc=s_t)
                                E("vector", lambda e, sub=sub, o=o, pb=pb: e.tensor_copy(
                                    outb4[:, sub * D + o * 128:sub * D + (o + 1) * 128],
                                    ps[pb][:, 0:128]), waits=[s_t, s_out], inc=s_v)
                        for sub in range(NTT):
                            ti = i * NTT + sub
                            xr = xrb[:, (sub % 2) * D:(sub % 2 + 1) * D]
                            dma("gpsimd", xr, xk[ti * 128:(ti + 1) * 128, :],
                                s_ld, waits=[s_v] if il == 0 and sub < 2
                                else [(s_v, fx_slot_wm[sub % 2])])
                            ob = outb4[:, sub * D:(sub + 1) * D]
                            E("vector", lambda e, ti=ti: e.tensor_scalar_add(
                                st8[:, 0:1], stats[:, ti * 8 + 1:ti * 8 + 2], 1.0))
                            E("vector", lambda e, ti=ti: e.tensor_tensor(
                                dltw[:, :].rearrange("p (t j) -> p t j", j=32),
                                delta[:, ti * 32:(ti + 1) * 32]
                                .unsqueeze(1).broadcast_to([128, 64, 32]),
                                oneh[:, ti * 64:(ti + 1) * 64]
                                .unsqueeze(2).broadcast_to([128, 64, 32]),
                                ALU.mult))
                            E("vector", lambda e, xr=xr, ob=ob: e.scalar_tensor_tensor(
                                scrA[:, 0:D], xr, st8[:, 0:1], ob, ALU.mult, ALU.add),
                                waits=[s_ld])
                            E("vector", lambda e, ti=ti, ob=ob: e.scalar_tensor_tensor(
                                ob, dltw[:, :], stats[:, ti * 8 + 7:ti * 8 + 8],
                                scrA[:, 0:D], ALU.add, ALU.add), inc=s_v)
                            fx_slot_wm[sub % 2] = s_v.n
                            dma("gpsimd", y[ti * 128:(ti + 1) * 128, :], ob, s_out,
                                waits=[s_v])
                        flush()

        def fin(e):
            return e.nop()
        E("gpsimd", fin, waits=[s_out])
        flush()

    return nc, taps


_PROG_CACHE = {}


def _host_tables():
    half = HD // 2
    j = np.arange(half, dtype=np.float64)
    invf = 1.0 / (10000.0 ** (j / half))
    pos = np.arange(T, dtype=np.float64)
    ang = pos[None, :] * invf[:, None]
    cos = np.cos(ang)
    sin = np.sin(ang)
    cosk = np.concatenate([cos, cos], axis=0).astype(np.float32)
    sink = np.concatenate([-sin, sin], axis=0).astype(np.float32)
    tri = np.where(np.arange(128)[None, :] >= np.arange(128)[:, None],
                   0.0, KILL).astype(np.float32)
    ident = np.eye(128, dtype=np.float32)
    return cosk, sink, tri, ident


def kernel(L0_attn_x, L0_attn_Wq, L0_attn_Wk, L0_attn_Wv, L0_attn_Wp,
           L0_ffn_ln_gamma, L0_ffn_ln_beta, L0_ffn_spline_bases,
           L0_ffn_spline_slopes, L0_ffn_output_scale):
    x = np.ascontiguousarray(np.asarray(L0_attn_x, dtype=np.float32))
    Wq = np.asarray(L0_attn_Wq, dtype=np.float32)
    Wk = np.asarray(L0_attn_Wk, dtype=np.float32)
    Wv = np.asarray(L0_attn_Wv, dtype=np.float32)
    Wp = np.asarray(L0_attn_Wp, dtype=np.float32)

    wqc = np.ascontiguousarray(
        Wq.T.reshape(D, 16, 128).transpose(1, 0, 2))
    wpc = np.ascontiguousarray(
        Wp.T.reshape(D, 16, 128).transpose(1, 0, 2))
    wkc = np.ascontiguousarray(Wk.T)
    wvc = np.ascontiguousarray(Wv.T)
    basesc = np.ascontiguousarray(
        np.asarray(L0_ffn_spline_bases, dtype=np.float32).reshape(64, 512))
    slopesc = np.ascontiguousarray(
        np.asarray(L0_ffn_spline_slopes, dtype=np.float32).reshape(64, 512))
    gammac = np.ascontiguousarray(np.asarray(L0_ffn_ln_gamma, dtype=np.float32))
    betac = np.ascontiguousarray(np.asarray(L0_ffn_ln_beta, dtype=np.float32))
    oscalec = np.ascontiguousarray(np.asarray(L0_ffn_output_scale, dtype=np.float32))

    cosk, sink, tri, ident = _host_tables()

    key = "prog"
    if key not in _PROG_CACHE:
        _PROG_CACHE[key] = build_program()
    nc, taps = _PROG_CACHE[key]

    in_maps = []
    perms = []
    for c in range(8):
        b = c // 2
        par = c % 2
        own = OWN_SLOTS[par]
        other = [s for s in range(N_ST) if s not in own]
        order = own + other
        tokperm = np.concatenate(
            [np.arange(s * ST, (s + 1) * ST) for s in order])
        perms.append(tokperm)
        xkc = np.ascontiguousarray(x[b][tokperm])
        coskc = np.ascontiguousarray(cosk[:, tokperm])
        sinkc = np.ascontiguousarray(sink[:, tokperm])
        bt = np.full((4, 4), KILL, dtype=np.float32)
        for i in range(4):
            for m in range(4):
                if m < len(other) and other[m] < own[i]:
                    bt[i, m] = -BSHIFT
        btc = np.ascontiguousarray(
            np.broadcast_to(bt.reshape(1, 16), (128, 16)).astype(np.float32))
        in_maps.append({
            "xk": xkc, "wq": wqc, "wk": wkc, "wv": wvc, "wp": wpc,
            "gamma": gammac, "beta": betac, "bases": basesc,
            "slopes": slopesc, "oscale": oscalec,
            "cosk": coskc, "sink": sinkc, "trimask": tri, "ident": ident,
            "btab": btc,
        })

    res = run_bass_kernel_spmd(nc, in_maps, core_ids=list(range(8)))

    out = np.empty((B, T, D), dtype=np.float32)
    for c in range(8):
        b = c // 2
        yv = res.results[c]["y"]
        out[b].reshape(T, D)[perms[c][:OWN_ST * ST]] = yv
    if TAPS:
        kernel.tap_results = [
            {k: res.results[c].get("tap_" + k) for k in TAPS} for c in range(8)]
    return out


# version 18
# speedup vs baseline: 1.0045x; 1.0045x over previous
import numpy as np
from contextlib import ExitStack

import concourse.bass as bass
import concourse.mybir as mybir
from concourse.bass_utils import run_bass_kernel_spmd

F32 = mybir.dt.float32
BF16 = mybir.dt.bfloat16
I32 = mybir.dt.int32
AF = mybir.ActivationFunctionType
ALU = mybir.AluOpType
AX = mybir.AxisListType

B, T, D = 4, 4096, 2048
ST = 512
N_ST = T // ST
OWN_ST = 4
NTT = ST // 128
N_OWN_TT = OWN_ST * NTT
KVH, QH, HD = 4, 16, 128
DCH = D // 128
GRID = 16
EPS = 1e-5
BSHIFT = 12.0
KILL = -1e9
SCALE_QK = float(1.0 / np.sqrt(HD))
P_OTHER = [1, 2, 3, 4]
BIGF = 1e30
EPSI = float(np.float32(0.0009765625))
CLIP16 = float(np.float32(np.float32(1.0 - 1e-6) * np.float32(16.0)))
OWN_SLOTS = ([0, 3, 4, 7], [1, 2, 5, 6])

TAPS = []
CC_CORES = 8
PHASE_LIMIT = 6
P1_ST = 8
P1_KV = True
P1_FFN = True
P1_TT = 4


def build_program():
    nc = bass.Bass("TRN2", target_bir_lowering=False, debug=False, detect_race_conditions=False)

    xk = nc.dram_tensor("xk", [T, D], F32, kind="ExternalInput").ap()
    wqd = nc.dram_tensor("wq", [16, D, 128], F32, kind="ExternalInput").ap()
    wkd = nc.dram_tensor("wk", [D, 512], F32, kind="ExternalInput").ap()
    wvd = nc.dram_tensor("wv", [D, 512], F32, kind="ExternalInput").ap()
    wpd = nc.dram_tensor("wp", [16, D, 128], F32, kind="ExternalInput").ap()
    gammad = nc.dram_tensor("gamma", [D], F32, kind="ExternalInput").ap()
    betad = nc.dram_tensor("beta", [D], F32, kind="ExternalInput").ap()
    basesd = nc.dram_tensor("bases", [64, 512], F32, kind="ExternalInput").ap()
    slopesd = nc.dram_tensor("slopes", [64, 512], F32, kind="ExternalInput").ap()
    oscaled = nc.dram_tensor("oscale", [64], F32, kind="ExternalInput").ap()
    coskd = nc.dram_tensor("cosk", [128, T], F32, kind="ExternalInput").ap()
    sinkd = nc.dram_tensor("sink", [128, T], F32, kind="ExternalInput").ap()
    trimd = nc.dram_tensor("trimask", [128, 128], F32, kind="ExternalInput").ap()
    identd = nc.dram_tensor("ident", [128, 128], F32, kind="ExternalInput").ap()
    btabd = nc.dram_tensor("btab", [128, 16], F32, kind="ExternalInput").ap()
    y = nc.dram_tensor("y", [OWN_ST * ST, D], F32, kind="ExternalOutput").ap()

    xnt_spill = nc.dram_tensor("xnt_spill", [OWN_ST, DCH, 128, ST], BF16).ap()
    cc_in_t = nc.dram_tensor("cc_in", [1, 128], F32)
    cc_out_t = nc.dram_tensor("cc_out", [1, 128], F32)
    cc_in = cc_in_t.ap()
    cc_out = cc_out_t.ap()

    taps = {}
    def tap(name, shape, dt=F32):
        if name in TAPS:
            taps[name] = nc.dram_tensor("tap_" + name, shape, dt,
                                        kind="ExternalOutput").ap()
            return taps[name]
        return None

    class Ctr:
        def __init__(self, h):
            self.h = h
            self.n = 0

    pend = {e: [] for e in ("sync", "gpsimd", "vector", "scalar", "tensor")}

    def E(eng, fn, waits=(), inc=None, inc_k=1, rb=None):
        wl = []
        for w in waits:
            if isinstance(w, tuple):
                if w[1] > 0:
                    wl.append((w[0].h, w[1]))
            elif w.n > 0:
                wl.append((w.h, w.n))
        if inc is not None:
            inc.n += inc_k
        pend[eng].append((fn, wl, None if inc is None else inc.h, inc_k, rb))

    def dma(eng, dst, src, ctr, waits=(), transpose=False):
        def f(e, d=dst, s=src, tr=transpose):
            if tr:
                return e.dma_start(out=d, in_=s, transpose=True)
            return e.dma_start(out=d, in_=s)
        E(eng, f, waits=waits, inc=ctr, inc_k=16)

    with ExitStack() as stack:
        def sb(name, shape, dt, st=stack):
            return st.enter_context(nc.sbuf_tensor(name, shape, dt))

        def psb(name, shape, st=stack):
            return st.enter_context(nc.psum_tensor(name, shape, F32))

        def sem(name, st=stack):
            return Ctr(st.enter_context(nc.semaphore(name)))

        block = stack.enter_context(nc.Block())

        def flush():
            for eng in ("sync", "gpsimd", "vector", "scalar", "tensor"):
                lst = pend[eng]
                if not lst:
                    continue
                pend[eng] = []

                def body(e, lst=lst, eng=eng):
                    for fn, wl, sem_h, k, rb in lst:
                        for h, v in wl:
                            e.wait_ge(h, v)
                        ins = fn(e)
                        if eng == "vector" and ins is not None:
                            ins.then_inc(s_vv.h, 1)
                            s_vv.n += 1
                            e.wait_ge(s_vv.h, s_vv.n)
                            if sem_h is not None:
                                e.nop().then_inc(sem_h, k)
                        elif eng == "scalar" and rb is not None:
                            rbi = e.copy(jnk[0:rb.shape[0], 0:1], rb)
                            if sem_h is not None:
                                rbi.then_inc(sem_h, k)
                        else:
                            if sem_h is not None:
                                if ins is None:
                                    ins = e.nop()
                                ins.then_inc(sem_h, k)

                getattr(block, eng)(body)

        KT = sb("KT", [128, KVH * T], BF16)
        V = sb("V", [128, (T // 128) * 512], BF16)
        tsig = sb("tsig", [128, DCH * 64], BF16)
        basesb = sb("basesb", [64, 512], F32)
        slopesb = sb("slopesb", [64, 512], F32)
        oscb = sb("oscb", [128, 64], F32)
        trim = sb("trim", [128, 128], BF16)
        idn = sb("idn", [128, 128], F32)
        identb = sb("identb", [128, 128], BF16)
        btabb = sb("btabb", [128, 16], F32)
        ones1b = sb("ones1b", [128, 1], BF16)
        ones1f = sb("ones1f", [1, 128], F32)
        cbias = sb("cbias", [128, 1], F32)
        negbig = sb("negbig", [128, 1], F32)
        gfull = sb("gfull", [128, 512], BF16)
        oneh = sb("oneh", [128, N_OWN_TT * 64], F32)
        onehi = sb("onehi", [128, 64], I32)
        xsl = sb("xsl", [128, N_OWN_TT * 32], F32)
        delta = sb("delta", [128, N_OWN_TT * 32], F32)
        stats = sb("stats", [128, N_OWN_TT * 8], F32)
        mmacc = sb("mmacc", [128, 1], F32)
        jnk = sb("jnk", [128, 1], F32)
        tminr = sb("tminr", [128, 64], F32)
        tmaxr = sb("tmaxr", [128, 64], F32)
        scrA = sb("scrA", [128, 2048], F32)
        sml = sb("sml", [128, 128], F32)
        sml2 = sb("sml2", [128, 64], F32)
        sstT = sb("sstT", [32, 64], BF16)
        st8 = sb("st8", [128, 8], F32)

        ps = [psb(f"ps{i}", [128, 512]) for i in range(8)]

        s_ld = sem("s_ld")
        s_ldx = sem("s_ldx")
        s_tr = sem("s_tr")
        s_v = sem("s_v")
        s_s = sem("s_s")
        s_t = sem("s_t")
        s_sp = sem("s_sp")
        s_g = sem("s_g")
        s_cc = sem("s_cc")
        s_out = sem("s_out")
        s_xa = sem("s_xa")
        s_xb = sem("s_xb")
        s_vv = sem("s_vv")

        def soft_barrier():
            allsems = [s_ld, s_ldx, s_tr, s_v, s_s, s_t, s_sp, s_g, s_cc,
                       s_out, s_xa, s_xb]
            for eng in ("sync", "gpsimd", "vector", "scalar", "tensor"):
                E(eng, lambda e: None, waits=list(allsems))

        def _finish_early():
            soft_barrier()
            flush()

        dma("gpsimd", basesb[:, :], basesd, s_ld)
        dma("gpsimd", slopesb[:, :], slopesd, s_ld)
        dma("gpsimd", oscb[:, :], oscaled.unsqueeze(0).broadcast_to([128, 64]), s_ld)
        dma("gpsimd", trim[:, :], trimd, s_ld)
        dma("gpsimd", idn[:, :], identd, s_ld)
        dma("gpsimd", btabb[:, :], btabd, s_ld)
        E("gpsimd", lambda e: e.iota(gfull[:, :], pattern=[[0, 32], [1, 16]], base=0,
                                     channel_multiplier=0,
                                     allow_small_or_imprecise_dtypes=True), inc=s_g)
        E("vector", lambda e: e.memset(ones1b[:, :], 1.0))
        E("vector", lambda e: e.memset(ones1f[:, :], 1.0))
        E("vector", lambda e: e.memset(cbias[:, :], -BSHIFT))
        E("vector", lambda e: e.memset(negbig[:, :], -BIGF))
        E("vector", lambda e: e.memset(mmacc[:, :], -BIGF))
        E("vector", lambda e: e.tensor_copy(identb[:, :], idn[:, :]),
          waits=[s_ld])
        E("vector", lambda e: e.memset(tsig[:, :], 0.0), inc=s_v)
        E("vector", lambda e: e.tensor_reduce(
            sml[0:64, 0:32], slopesb[:, :].rearrange("p (j g) -> p j g", g=GRID),
            AX.X, ALU.add), waits=[s_ld], inc=s_v)
        E("scalar", lambda e: e.activation(sml2[0:64, 0:32], sml[0:64, 0:32], AF.Sign),
          waits=[s_v, s_ld], inc=s_s, rb=sml2[0:64, 31:32])
        E("tensor", lambda e: e.transpose(ps[5][0:32, 0:64], sml2[0:64, 0:32],
                                          idn[0:64, 0:64]), waits=[s_s], inc=s_t)
        E("vector", lambda e: e.tensor_copy(sstT[:, :], ps[5][0:32, 0:64]),
          waits=[s_t], inc=s_v)
        for a in range(4):
            def place(e, a=a):
                return e.tensor_copy(
                    tsig[32 * a:32 * a + 32, a:a + 68 * 15 + 1:68],
                    sstT[0:32, a:a + 61:4])
            E("vector", place, waits=[s_v], inc=s_v)

        flush()

        with ExitStack() as p1:
            wkb = sb("wkb", [128, DCH * 512], BF16, p1)
            wvb = sb("wvb", [128, DCH * 512], BF16, p1)
            gam = sb("gam", [128, D], F32, p1)
            bet = sb("bet", [128, D], F32, p1)
            xbuf = sb("xbuf", [128, 2 * D], F32, p1)
            xnf = sb("xnf", [128, D], F32, p1)
            xn2 = sb("xn2", [128, D], F32, p1)
            sgn = sb("sgn", [128, D], BF16, p1)
            sgnT = sb("sgnT", [128, DCH * 128], BF16, p1)
            xnT = sb("xnT", [128, DCH * 512], BF16, p1)
            coss = sb("coss", [128, 512], F32, p1)
            sins = sb("sins", [128, 512], F32, p1)

            dma("gpsimd", wkb[:, :].rearrange("p (c f) -> p c f", c=DCH),
                wkd.rearrange("(c p) f -> p c f", p=128), s_ld)
            dma("gpsimd", wvb[:, :].rearrange("p (c f) -> p c f", c=DCH),
                wvd.rearrange("(c p) f -> p c f", p=128), s_ld)
            dma("gpsimd", gam[:, :], gammad.unsqueeze(0).broadcast_to([128, D]), s_ld)
            dma("gpsimd", bet[:, :], betad.unsqueeze(0).broadcast_to([128, D]), s_ld)

            t_x0raw = tap("x0raw", [128, D])
            t_xn0 = tap("xn0", [128, D])
            t_xn20 = tap("xn20", [128, D])
            t_ts0 = tap("ts0", [128, 64])
            t_oh0 = tap("oh0", [128, 64])
            t_kt = tap("kt", [128, KVH * T])
            t_v = tap("v", [128, (T // 128) * 512])
            t_xsl0 = tap("xsl0", [128, 32])
            t_stats = tap("statsq", [128, N_OWN_TT * 8])
            t_mm = tap("mmacc", [128, 1])

            NG = N_ST * NTT
            pe_wm = {4: (s_v, 0), 5: (s_v, 0),
                     6: (s_v, 0), 7: (s_v, 0)}
            xnf_free_wm = [0]
            xdone = [0] * (NG + 1)
            xfree_v = [0] * (NG + 1)
            xfree_s = [0] * (NG + 1)

            def load_x(g):
                if g >= NG:
                    return
                xb_off = (g % 2) * D
                w = []
                if g >= 2:
                    w = [(s_v, xfree_v[g - 2]), (s_s, xfree_s[g - 2])]
                slot_sem = s_xa if g % 2 == 0 else s_xb
                dma("gpsimd", xbuf[:, xb_off:xb_off + D],
                    xk[g * 128:(g + 1) * 128, :], slot_sem, waits=w)
                xdone[g] = (slot_sem, slot_sem.n)

            load_x(0)
            for st_i in range(min(N_ST, P1_ST)):
                own = st_i < OWN_ST
                cs_off = 0
                dma("gpsimd", coss[:, cs_off:cs_off + 512],
                    coskd[:, st_i * 512:(st_i + 1) * 512], s_ldx, waits=[s_v])
                dma("gpsimd", sins[:, cs_off:cs_off + 512],
                    sinkd[:, st_i * 512:(st_i + 1) * 512], s_ldx, waits=[s_v])

                for tt in range(min(NTT, P1_TT)):
                    g = st_i * NTT + tt
                    ti = g
                    xb_off = (g % 2) * D
                    xv = xbuf[:, xb_off:xb_off + D]
                    if t_x0raw is not None and g == 0:
                        E("vector", lambda e, xv=xv: e.tensor_copy(scrA[:, :], xv),
                          waits=[xdone[g]], inc=s_v)
                        dma("gpsimd", t_x0raw, scrA[:, :], s_sp, waits=[s_v])
                    E("vector", lambda e, xv=xv: e.reduce_sum(
                        sml[:, 0:64], xv.rearrange("p (a b) -> p a b", a=64), AX.X),
                      waits=[xdone[g], s_s])
                    E("vector", lambda e: e.reduce_sum(st8[:, 0:1], sml[:, 0:64], AX.X))
                    E("vector", lambda e: e.tensor_scalar_mul(st8[:, 2:3], st8[:, 0:1],
                                                              1.0 / D))
                    E("vector", lambda e, xv=xv: e.tensor_single_scalar(
                        xnf[:, :], xv, st8[:, 2:3], ALU.subtract),
                      waits=[(s_t, xnf_free_wm[0])])
                    E("vector", lambda e: e.tensor_tensor(
                        xn2[:, :], xnf[:, :], xnf[:, :], ALU.mult))
                    E("vector", lambda e: e.reduce_sum(
                        sml[:, 0:64], xn2[:, :].rearrange("p (a b) -> p a b", a=64),
                        AX.X))
                    E("vector", lambda e: e.reduce_sum(st8[:, 1:2], sml[:, 0:64], AX.X))
                    xfree_s[g] = s_s.n
                    E("vector", lambda e: e.tensor_scalar(st8[:, 3:4], st8[:, 1:2],
                                                          1.0 / D, EPS, ALU.mult,
                                                          ALU.add), inc=s_v)
                    E("scalar", lambda e: e.activation(st8[:, 5:6], st8[:, 3:4], AF.Sqrt),
                      waits=[s_v], inc=s_s, rb=st8[:, 5:6])
                    E("vector", lambda e: e.reciprocal(st8[:, 6:7], st8[:, 5:6]),
                      waits=[s_s])
                    E("vector", lambda e: e.tensor_tensor(st8[:, 7:8], st8[:, 6:7],
                                                          st8[:, 6:7], ALU.mult))
                    E("vector", lambda e: e.tensor_tensor(st8[:, 7:8], st8[:, 3:4],
                                                          st8[:, 7:8], ALU.mult))
                    E("vector", lambda e: e.tensor_scalar(st8[:, 7:8], st8[:, 7:8],
                                                          -0.5, 1.5, ALU.mult, ALU.add))
                    E("vector", lambda e: e.tensor_tensor(st8[:, 4:5], st8[:, 6:7],
                                                          st8[:, 7:8], ALU.mult))
                    E("vector", lambda e: e.tensor_scalar_mul(
                        xnf[:, :], xnf[:, :], st8[:, 4:5]), inc=s_v)
                    xfree_v[g] = s_v.n
                    load_x(g + 1)
                    if own:
                        E("vector", lambda e, ti=ti: e.tensor_copy(
                            stats[:, ti * 8:ti * 8 + 1], st8[:, 2:3]))
                        E("vector", lambda e, ti=ti: e.tensor_copy(
                            stats[:, ti * 8 + 1:ti * 8 + 2], st8[:, 4:5]))
                        E("vector", lambda e, ti=ti: e.tensor_tensor(
                            stats[:, ti * 8 + 7:ti * 8 + 8], st8[:, 2:3], st8[:, 4:5],
                            ALU.mult))
                        E("vector", lambda e, ti=ti: e.tensor_scalar_mul(
                            stats[:, ti * 8 + 7:ti * 8 + 8],
                            stats[:, ti * 8 + 7:ti * 8 + 8], -1.0), inc=s_v)
                    if t_xn0 is not None and g == 0:
                        E("vector", lambda e: e.tensor_copy(scrA[:, :], xnf[:, :]),
                          waits=[s_sp], inc=s_v)
                        dma("gpsimd", t_xn0, scrA[:, :], s_sp, waits=[s_v])
                    for q in range(4):
                        bk = 4 + q
                        for j in range(4):
                            c = q * 4 + j
                            dw = [(s_v, xfree_v[g])] if q == 0 and j == 0 else []
                            E("tensor", lambda e, c=c, j=j, bk=bk: e.transpose(
                                ps[bk][:, j * 128:(j + 1) * 128],
                                xnf[:, c * 128:(c + 1) * 128], idn[:, :]),
                                waits=dw + ([pe_wm[bk]] if j == 0 else []),
                                inc=s_t)
                        twm = s_t.n
                        ew = [(s_t, twm)]
                        if q == 0 and tt == 0:
                            ew.append(s_sp)
                        E("vector", lambda e, q=q, tt=tt, bk=bk: e.tensor_copy(
                            xnT[:, :].rearrange("p (c w) -> p c w", w=512)[
                                :, q * 4:(q + 1) * 4, tt * 128:(tt + 1) * 128],
                            ps[bk][:, :].rearrange("p (j w) -> p j w", w=128)),
                            waits=ew, inc=s_v)
                        pe_wm[bk] = (s_v, s_v.n)
                    xnf_free_wm[0] = s_t.n
                    last_xnT_ev = s_v.n

                    if own and P1_FFN:
                        E("vector", lambda e: e.reduce_sum(
                            sml[:, 0:64], xnf[:, :].rearrange("p (a b) -> p a b", a=64),
                            AX.X))
                        E("vector", lambda e: e.reduce_sum(st8[:, 0:1], sml[:, 0:64],
                                                           AX.X))
                        E("vector", lambda e: e.tensor_scalar_mul(st8[:, 2:3],
                                                                  st8[:, 0:1], 1.0 / D))
                        E("vector", lambda e: e.tensor_single_scalar(
                            scrA[:, :], xnf[:, :], st8[:, 2:3], ALU.subtract))
                        E("vector", lambda e: e.tensor_tensor(
                            xn2[:, :], scrA[:, :], scrA[:, :], ALU.mult))
                        E("vector", lambda e: e.reduce_sum(
                            sml[:, 0:64], xn2[:, :].rearrange("p (a b) -> p a b", a=64),
                            AX.X))
                        E("vector", lambda e: e.reduce_sum(st8[:, 1:2], sml[:, 0:64],
                                                           AX.X))
                        E("vector", lambda e: e.tensor_scalar(st8[:, 3:4], st8[:, 1:2],
                                                              1.0 / D, EPS, ALU.mult,
                                                              ALU.add), inc=s_v)
                        E("scalar", lambda e: e.activation(st8[:, 5:6], st8[:, 3:4],
                                                           AF.Sqrt),
                          waits=[s_v], inc=s_s, rb=st8[:, 5:6])
                        E("vector", lambda e: e.reciprocal(st8[:, 6:7], st8[:, 5:6]),
                          waits=[s_s])
                        E("vector", lambda e: e.tensor_tensor(st8[:, 7:8], st8[:, 6:7],
                                                              st8[:, 6:7], ALU.mult))
                        E("vector", lambda e: e.tensor_tensor(st8[:, 7:8], st8[:, 3:4],
                                                              st8[:, 7:8], ALU.mult))
                        E("vector", lambda e: e.tensor_scalar(st8[:, 7:8], st8[:, 7:8],
                                                              -0.5, 1.5, ALU.mult,
                                                              ALU.add))
                        E("vector", lambda e: e.tensor_tensor(st8[:, 4:5], st8[:, 6:7],
                                                              st8[:, 7:8], ALU.mult))
                        E("vector", lambda e: e.scalar_tensor_tensor(
                            xn2[:, :], scrA[:, :], st8[:, 4:5], gam[:, :], ALU.mult,
                            ALU.mult), waits=[s_ld])
                        E("vector", lambda e: e.tensor_add(xn2[:, :], xn2[:, :], bet[:, :]),
                          inc=s_v)
                        if t_xn20 is not None and g == 0:
                            dma("gpsimd", t_xn20, xn2[:, :], s_sp, waits=[s_v])
                        E("scalar", lambda e: e.activation(scrA[:, :], xn2[:, :], AF.Sign),
                          waits=[s_v], inc=s_s, rb=scrA[:, D - 1:D])
                        sign_wm = s_s.n
                        for q in range(4):
                            bk = 4 + q
                            for j in range(4):
                                c = q * 4 + j
                                dw = [(s_s, sign_wm)] if q == 0 and j == 0 else []
                                E("tensor", lambda e, c=c, j=j, bk=bk: e.transpose(
                                    ps[bk][:, j * 128:(j + 1) * 128],
                                    scrA[:, c * 128:(c + 1) * 128], idn[:, :]),
                                    waits=dw + ([pe_wm[bk]] if j == 0 else []),
                                    inc=s_t)
                            twm = s_t.n
                            E("vector", lambda e, q=q, bk=bk: e.tensor_copy(
                                sgnT[:, q * 512:(q + 1) * 512], ps[bk][:, :]),
                                waits=[(s_t, twm)], inc=s_v)
                            pe_wm[bk] = (s_v, s_v.n)
                        for c in range(DCH):
                            E("tensor", lambda e, c=c: e.matmul(
                                ps[4][:, 0:64], sgnT[:, c * 128:(c + 1) * 128],
                                tsig[:, c * 64:(c + 1) * 64],
                                start=(c == 0), stop=(c == DCH - 1)),
                                waits=[s_s, s_v] if c == 0 else [],
                                inc=s_t if c == DCH - 1 else None)
                        E("vector", lambda e: e.tensor_copy(scrA[:, 1984:2048], ps[4][:, 0:64]),
                          waits=[s_t], inc=s_v)
                        if t_ts0 is not None and g == 0:
                            dma("gpsimd", t_ts0, scrA[:, 1984:2048], s_sp, waits=[s_v])
                        E("vector", lambda e: e.tensor_reduce(
                            sml[:, 0:8], scrA[:, 1984:2048].rearrange("p (c t) -> p c t", t=8),
                            AX.X, ALU.add))
                        E("vector", lambda e: e.scalar_tensor_tensor(
                            sml[:, 8:16], gfull[:, 0:8], -EPSI, sml[:, 0:8],
                            ALU.mult, ALU.add), waits=[s_g])
                        E("vector", lambda e: e.tensor_reduce(
                            sml[:, 16:17], sml[:, 8:16], AX.X, ALU.max))
                        E("vector", lambda e: e.tensor_tensor(
                            sml[:, 24:32], sml[:, 8:16],
                            sml[:, 16:17].broadcast_to([128, 8]), ALU.is_ge))
                        E("vector", lambda e: e.scalar_tensor_tensor(
                            sml2[:, 0:64], gfull[:, 0:64], -EPSI, scrA[:, 1984:2048],
                            ALU.mult, ALU.add))
                        E("vector", lambda e: e.scalar_tensor_tensor(
                            sml2[:, 0:64].rearrange("p (c t) -> p c t", t=8),
                            sml[:, 24:32].unsqueeze(2).broadcast_to([128, 8, 8]),
                            512.0,
                            sml2[:, 0:64].rearrange("p (c t) -> p c t", t=8),
                            ALU.mult, ALU.add))
                        E("vector", lambda e: e.tensor_reduce(
                            sml[:, 32:33], sml2[:, 0:64], AX.X, ALU.max))
                        E("vector", lambda e, ti=ti: e.tensor_tensor(
                            oneh[:, ti * 64:(ti + 1) * 64], sml2[:, 0:64],
                            sml[:, 32:33].broadcast_to([128, 64]), ALU.is_ge))
                        E("vector", lambda e, ti=ti: e.tensor_copy(
                            onehi[:, :], oneh[:, ti * 64:(ti + 1) * 64]), inc=s_v)
                        if t_oh0 is not None and g == 0:
                            dma("gpsimd", t_oh0, oneh[:, 0:64], s_sp, waits=[s_v])
                        E("vector", lambda e: e.tensor_reduce(
                            sml[:, 0:64], xn2[:, :].rearrange("p (t j) -> p t j", j=32),
                            AX.X, ALU.min))
                        E("vector", lambda e: e.tensor_reduce(
                            sml[:, 64:128], xn2[:, :].rearrange("p (t j) -> p t j", j=32),
                            AX.X, ALU.max))
                        for (src_off, dst_slot) in ((0, 2), (64, 3)):
                            E("vector", lambda e, ti=ti, so=src_off: e.tensor_tensor(
                                sml2[:, 0:64], oneh[:, ti * 64:(ti + 1) * 64],
                                sml[:, so:so + 64], ALU.mult))
                            E("vector", lambda e, ti=ti, sl=dst_slot: e.tensor_reduce(
                                stats[:, ti * 8 + sl:ti * 8 + sl + 1], sml2[:, 0:64],
                                AX.X, ALU.add))
                        E("vector", lambda e, ti=ti: e.tensor_tensor(
                            sml2[:, 0:64], oneh[:, ti * 64:(ti + 1) * 64], oscb[:, :],
                            ALU.mult))
                        E("vector", lambda e, ti=ti: e.tensor_reduce(
                            stats[:, ti * 8 + 4:ti * 8 + 5], sml2[:, 0:64], AX.X, ALU.add))
                        E("vector", lambda e, ti=ti: e.tensor_tensor(
                            scrA[:, :].rearrange("p (t j) -> p t j", j=32),
                            xn2[:, :].rearrange("p (t j) -> p t j", j=32),
                            oneh[:, ti * 64:(ti + 1) * 64].unsqueeze(2).broadcast_to(
                                [128, 64, 32]), ALU.mult))
                        E("vector", lambda e, ti=ti: e.tensor_reduce(
                            xsl[:, ti * 32:(ti + 1) * 32],
                            scrA[:, :].rearrange("p (t j) -> p j t", t=64),
                            AX.X, ALU.add), inc=s_v)
                        if t_xsl0 is not None and g == 0:
                            dma("gpsimd", t_xsl0, xsl[:, 0:32], s_sp, waits=[s_v])
                        E("vector", lambda e, ti=ti: e.tensor_scalar_mul(
                            st8[:, 6:7], stats[:, ti * 8 + 2:ti * 8 + 3], -1.0))
                        E("vector", lambda e: e.select(
                            sml[:, 0:64], onehi[:, :],
                            st8[:, 6:7].broadcast_to([128, 64]),
                            negbig[:, :].broadcast_to([128, 64])))
                        E("vector", lambda e, ti=ti: e.select(
                            sml[:, 64:128], onehi[:, :],
                            stats[:, ti * 8 + 3:ti * 8 + 4].broadcast_to([128, 64]),
                            negbig[:, :].broadcast_to([128, 64])), inc=s_v)
                        E("tensor", lambda e: e.transpose(ps[5][:, 0:128], sml[:, :],
                                                          idn[:, :]),
                          waits=[s_v], inc=s_t)
                        E("vector", lambda e: e.tensor_reduce(
                            sml2[:, 0:1], ps[5][:, 0:128], AX.X, ALU.max), waits=[s_t])
                        E("vector", lambda e: e.tensor_max(
                            mmacc[:, :], mmacc[:, :], sml2[:, 0:1]), inc=s_v)

                for h4 in range(KVH if P1_KV else 0):
                    bank = h4 % 2
                    for c in range(DCH):
                        E("tensor", lambda e, h4=h4, c=c, bank=bank: e.matmul(
                            ps[bank][:, :],
                            wkb[:, c * 512 + h4 * 128: c * 512 + (h4 + 1) * 128],
                            xnT[:, c * 512:(c + 1) * 512],
                            start=(c == 0), stop=(c == DCH - 1)),
                            waits=[s_ld, s_v] if c == 0 else [],
                            inc=s_t if c == DCH - 1 else None)
                    kt_off = h4 * T + st_i * 512
                    E("vector", lambda e, bank=bank, co=cs_off: e.tensor_tensor(
                        scrA[:, 0:512], ps[bank][:, :], coss[:, co:co + 512],
                        ALU.mult), waits=[s_t, s_ldx])
                    E("vector", lambda e, bank=bank, co=cs_off: e.tensor_tensor(
                        scrA[0:64, 512:1024], ps[bank][64:128, :],
                        sins[0:64, co:co + 512], ALU.mult))
                    E("vector", lambda e, bank=bank, co=cs_off: e.tensor_tensor(
                        scrA[64:128, 512:1024], ps[bank][0:64, :],
                        sins[64:128, co:co + 512], ALU.mult))
                    E("vector", lambda e, kt_off=kt_off: e.tensor_add(
                        KT[:, kt_off:kt_off + 512], scrA[:, 0:512],
                        scrA[:, 512:1024]), inc=s_v)
                for ksub in range(NTT if P1_KV else 0):
                    bank = 2 + ksub % 2
                    for c in range(DCH):
                        E("tensor", lambda e, ksub=ksub, c=c, bank=bank: e.matmul(
                            ps[bank][:, :],
                            xnT[:, c * 512 + ksub * 128: c * 512 + (ksub + 1) * 128],
                            wvb[:, c * 512:(c + 1) * 512],
                            start=(c == 0), stop=(c == DCH - 1)),
                            waits=[s_ld, s_v] if c == 0 else [],
                            inc=s_t if c == DCH - 1 else None)
                    v_off = (st_i * NTT + ksub) * 512
                    E("vector", lambda e, bank=bank, v_off=v_off: e.tensor_copy(
                        V[:, v_off:v_off + 512], ps[bank][:, :]), waits=[s_t], inc=s_v)

                if own:
                    dma("gpsimd", xnt_spill[st_i].rearrange("c p f -> p c f"),
                        xnT[:, :].rearrange("p (c f) -> p c f", c=DCH), s_sp,
                        waits=[(s_v, last_xnT_ev), s_t])
                flush()

            if t_kt is not None:
                dma("gpsimd", t_kt, KT[:, :], s_sp, waits=[s_v])
            if t_v is not None:
                dma("gpsimd", t_v, V[:, :], s_sp, waits=[s_v])
            if t_stats is not None:
                dma("gpsimd", t_stats, stats[:, :], s_sp, waits=[s_v])
            if t_mm is not None:
                dma("gpsimd", t_mm, mmacc[:, :], s_sp, waits=[s_v])
            flush()

        if PHASE_LIMIT <= 1:
            _finish_early()
            return nc, taps
        soft_barrier()
        dma("gpsimd", cc_in.rearrange("o f -> f o"), mmacc[:, :], s_sp, waits=[s_v])
        E("gpsimd", lambda e: e.collective_compute(
            "AllReduce", ALU.max, replica_groups=[list(range(CC_CORES))],
            ins=[cc_in.opt()], outs=[cc_out.opt()]),
            waits=[s_sp], inc=s_cc)
        dma("gpsimd", tminr[:, :], cc_out[0:1, 0:64].broadcast_to([128, 64]),
            s_ld, waits=[s_cc])
        dma("gpsimd", tmaxr[:, :], cc_out[0:1, 64:128].broadcast_to([128, 64]),
            s_ld, waits=[s_cc])
        E("vector", lambda e: e.tensor_scalar_mul(tminr[:, :], tminr[:, :], -1.0),
          waits=[s_ld], inc=s_v)
        t_tmin = tap("tmin", [128, 64])
        if t_tmin is not None:
            dma("gpsimd", t_tmin, tminr[:, :], s_sp, waits=[s_v])
        flush()

        if PHASE_LIMIT <= 2:
            _finish_early()
            return nc, taps
        soft_barrier()
        with ExitStack() as pc:
            onehT = sb("onehT", [64, N_OWN_TT * 128], F32, pc)
            scrB = sb("scrB", [128, 1024], F32, pc)
            t_d0 = tap("delta0", [128, 32])
            t_u0 = tap("u0", [128, 32])

            for ti in range(N_OWN_TT):
                ohv = oneh[:, ti * 64:(ti + 1) * 64]
                E("vector", lambda e, ohv=ohv: e.tensor_tensor(
                    scrB[:, 0:64], ohv, tminr[:, :], ALU.mult), waits=[s_v])
                E("vector", lambda e, ti=ti: e.tensor_reduce(
                    stats[:, ti * 8 + 5:ti * 8 + 6], scrB[:, 0:64], AX.X, ALU.add))
                E("vector", lambda e, ohv=ohv: e.tensor_tensor(
                    scrB[:, 0:64], ohv, tmaxr[:, :], ALU.mult))
                E("vector", lambda e: e.tensor_reduce(
                    st8[:, 0:1], scrB[:, 0:64], AX.X, ALU.add))
                E("vector", lambda e, ti=ti: e.tensor_sub(
                    st8[:, 1:2], st8[:, 0:1], stats[:, ti * 8 + 5:ti * 8 + 6]))
                E("vector", lambda e: e.tensor_scalar_add(st8[:, 1:2], st8[:, 1:2], 1e-8))
                E("vector", lambda e: e.reciprocal(st8[:, 2:3], st8[:, 1:2]))
                E("vector", lambda e: e.tensor_tensor(st8[:, 3:4], st8[:, 1:2],
                                                      st8[:, 2:3], ALU.mult))
                E("vector", lambda e: e.tensor_scalar(st8[:, 3:4], st8[:, 3:4],
                                                      -1.0, 2.0, ALU.mult, ALU.add))
                E("vector", lambda e, ti=ti: e.tensor_tensor(
                    stats[:, ti * 8 + 6:ti * 8 + 7], st8[:, 2:3], st8[:, 3:4],
                    ALU.mult), inc=s_v)
                E("vector", lambda e, ti=ti: e.tensor_scalar(
                    scrB[:, 0:32], xsl[:, ti * 32:(ti + 1) * 32],
                    stats[:, ti * 8 + 5:ti * 8 + 6], stats[:, ti * 8 + 6:ti * 8 + 7],
                    ALU.subtract, ALU.mult))
                E("vector", lambda e: e.tensor_scalar(
                    scrB[:, 0:32], scrB[:, 0:32], 16.0, CLIP16, ALU.mult, ALU.min))
                E("vector", lambda e: e.tensor_scalar_max(
                    scrB[:, 0:32], scrB[:, 0:32], 0.0), inc=s_v)
                if t_u0 is not None and ti == 0:
                    dma("gpsimd", t_u0, scrB[:, 0:32], s_sp, waits=[s_v])
                E("vector", lambda e: e.tensor_tensor(
                    scrB[:, 512:992].rearrange("p (j g) -> p j g", g=15),
                    scrB[:, 0:32].unsqueeze(2).broadcast_to([128, 32, 15]),
                    gfull[:, :].rearrange("p (j g) -> p j g", g=16)[:, :, 1:16],
                    ALU.is_ge), waits=[s_g])
                E("vector", lambda e: e.tensor_reduce(
                    scrB[:, 32:64], scrB[:, 512:992].rearrange("p (j g) -> p j g", g=15),
                    AX.X, ALU.add))
                E("vector", lambda e: e.tensor_sub(
                    scrB[:, 64:96], scrB[:, 0:32], scrB[:, 32:64]), inc=s_v)
                E("tensor", lambda e, ohv=ohv: e.transpose(
                    ps[5][0:64, 0:128], ohv, idn[:, :]), waits=[s_v], inc=s_t)
                E("vector", lambda e, ti=ti: e.tensor_copy(
                    onehT[:, ti * 128:(ti + 1) * 128], ps[5][0:64, 0:128]),
                    waits=[s_t], inc=s_v)
                E("tensor", lambda e, ti=ti: e.matmul(
                    ps[0][:, :], onehT[:, ti * 128:(ti + 1) * 128], basesb[:, :],
                    start=True, stop=True), waits=[s_v])
                E("tensor", lambda e, ti=ti: e.matmul(
                    ps[1][:, :], onehT[:, ti * 128:(ti + 1) * 128], slopesb[:, :],
                    start=True, stop=True), inc=s_t)
                E("vector", lambda e: e.tensor_tensor(
                    scrA[:, 0:512].rearrange("p (j g) -> p j g", g=16),
                    scrB[:, 32:64].unsqueeze(2).broadcast_to([128, 32, 16]),
                    gfull[:, :].rearrange("p (j g) -> p j g", g=16), ALU.is_equal),
                    waits=[s_t])
                E("vector", lambda e: e.tensor_tensor(
                    scrA[:, 512:1024], scrA[:, 0:512], ps[0][:, :], ALU.mult))
                E("vector", lambda e: e.tensor_reduce(
                    scrB[:, 96:128], scrA[:, 512:1024].rearrange("p (j g) -> p j g", g=16),
                    AX.X, ALU.add))
                E("vector", lambda e: e.tensor_tensor(
                    scrA[:, 512:1024], scrA[:, 0:512], ps[1][:, :], ALU.mult))
                E("vector", lambda e: e.tensor_reduce(
                    scrB[:, 128:160],
                    scrA[:, 512:1024].rearrange("p (j g) -> p j g", g=16),
                    AX.X, ALU.add), inc=s_v)
                E("vector", lambda e: e.tensor_tensor(
                    scrB[:, 160:192], scrB[:, 128:160], scrB[:, 64:96], ALU.mult))
                E("vector", lambda e: e.tensor_add(
                    scrB[:, 160:192], scrB[:, 160:192], scrB[:, 96:128]))
                E("vector", lambda e, ti=ti: e.tensor_scalar_mul(
                    delta[:, ti * 32:(ti + 1) * 32], scrB[:, 160:192],
                    stats[:, ti * 8 + 4:ti * 8 + 5]), inc=s_v)
                if t_d0 is not None and ti == 0:
                    dma("gpsimd", t_d0, delta[:, 0:32], s_sp, waits=[s_v])
            flush()

        t_qt = tap("qt", [128, QH * 2 * 512], BF16)
        t_ot = tap("ot", [128, QH * 2 * 512], BF16)
        t_att0 = tap("att0", [128, D])

        if PHASE_LIMIT <= 3:
            _finish_early()
            return nc, taps
        for g2 in range(2):
            soft_barrier()
            flush()
            with ExitStack() as pg:
                QO = sb(f"QO{g2}", [128, QH * 2 * 512], BF16, pg)
                wqb = sb(f"wqb{g2}", [128, DCH * 128], BF16, pg)
                wqb2 = sb(f"wqb2{g2}", [128, DCH * 128], BF16, pg)
                PT = sb(f"PT{g2}", [128, 4 * 512], BF16, pg)
                recb = sb(f"recb{g2}", [128, 512], F32, pg)
                denr = sb(f"denr{g2}", [1, 512], F32, pg)
                recr = sb(f"recr{g2}", [1, 512], F32, pg)

                with ExitStack() as pd:
                    xnTd = sb(f"xnTd{g2}", [128, 2 * DCH * 512], BF16, pd)
                    cosq = sb(f"cosq{g2}", [128, 2 * 512], F32, pd)
                    sinq = sb(f"sinq{g2}", [128, 2 * 512], F32, pd)
                    for il in range(2):
                        i = 2 * g2 + il
                        dma("gpsimd", xnTd[:, il * DCH * 512:(il + 1) * DCH * 512]
                            .rearrange("p (c f) -> p c f", c=DCH),
                            xnt_spill[i].rearrange("c p f -> p c f"), s_ld,
                            waits=[s_t, s_v])
                        dma("gpsimd", cosq[:, il * 512:(il + 1) * 512],
                            coskd[:, i * 512:(i + 1) * 512], s_ldx, waits=[s_v])
                        dma("gpsimd", sinq[:, il * 512:(il + 1) * 512],
                            sinkd[:, i * 512:(i + 1) * 512], s_ldx, waits=[s_v])
                    d_wm = {}
                    for h in range(QH):
                        wbuf = wqb if h % 2 == 0 else wqb2
                        dma("gpsimd", wbuf[:, :].rearrange("p (c f) -> p c f", c=DCH),
                            wqd[h].rearrange("(c p) f -> p c f", p=128), s_ld,
                            waits=[(s_t, d_wm[h - 2])] if h >= 2 else [s_t])
                        for il in range(2):
                            bank = il
                            for c in range(DCH):
                                E("tensor", lambda e, c=c, bank=bank, wbuf=wbuf, il=il: e.matmul(
                                    ps[bank][:, :], wbuf[:, c * 128:(c + 1) * 128],
                                    xnTd[:, il * DCH * 512 + c * 512:
                                         il * DCH * 512 + (c + 1) * 512],
                                    start=(c == 0), stop=(c == DCH - 1)),
                                    waits=[s_ld, s_v] if c == 0 else [],
                                    inc=s_t if c == DCH - 1 else None)
                            qo_off = (h * 2 + il) * 512
                            co = il * 512
                            E("vector", lambda e, bank=bank, co=co: e.tensor_tensor(
                                scrA[:, 0:512], ps[bank][:, :], cosq[:, co:co + 512],
                                ALU.mult), waits=[s_t, s_ldx])
                            E("vector", lambda e, bank=bank, co=co: e.tensor_tensor(
                                scrA[0:64, 512:1024], ps[bank][64:128, :],
                                sinq[0:64, co:co + 512], ALU.mult))
                            E("vector", lambda e, bank=bank, co=co: e.tensor_tensor(
                                scrA[64:128, 512:1024], ps[bank][0:64, :],
                                sinq[64:128, co:co + 512], ALU.mult))
                            E("vector", lambda e, qo_off=qo_off: e.tensor_add(
                                QO[:, qo_off:qo_off + 512], scrA[:, 0:512],
                                scrA[:, 512:1024]), inc=s_v)
                        d_wm[h] = s_t.n
                    flush()
                if t_qt is not None and g2 == 0:
                    dma("gpsimd", t_qt, QO[:, :], s_sp, waits=[s_v])
                    flush()

                if PHASE_LIMIT <= 4:
                    continue
                for il in range(2):
                    i = 2 * g2 + il
                    blocks = []
                    for m in range(P_OTHER[i]):
                        for b4 in range(4):
                            blocks.append(("oth", m, (16 + 4 * m + b4) * 128))
                    for j in range(4 * i):
                        blocks.append(("own", 0, j * 128))
                    for a in range(4):
                        blocks.append(("diag", a, (4 * i + a) * 128))
                    nb = len(blocks)
                    for h in range(QH):
                        ghead = h // 4
                        qo_off = (h * 2 + il) * 512
                        exp_wm = {}
                        qk_wm = {}
                        STB = [0, 1, 5, 6]

                        def emit_qk(bi):
                            kind, aux, kcol = blocks[bi]
                            bank = STB[bi % 4]
                            lo = 128 * aux if kind == "diag" else 0
                            if bi == 0:
                                w = [s_s, s_v]
                            elif bi >= 4:
                                w = [(s_s, exp_wm[bi - 4])]
                            else:
                                w = []
                            E("tensor", lambda e, bank=bank, kcol=kcol, lo=lo,
                              gh=ghead, qo=qo_off: e.matmul(
                                ps[bank][:, lo:512],
                                KT[:, gh * T + kcol:gh * T + kcol + 128],
                                QO[:, qo + lo:qo + 512], start=True,
                                stop=True), waits=w, inc=s_t)
                            qk_wm[bi] = s_t.n

                        def emit_exp(bi):
                            kind, aux, kcol = blocks[bi]
                            bank = STB[bi % 4]
                            ptile = bi % 4
                            lo = 128 * aux if kind == "diag" else 0
                            w = [(s_t, qk_wm[bi])]
                            if kind == "diag":
                                E("vector", lambda e, bank=bank, lo=lo: e.tensor_tensor(
                                    ps[bank][:, lo:lo + 128], ps[bank][:, lo:lo + 128],
                                    trim[:, :], ALU.add), waits=[(s_t, qk_wm[bi])],
                                    inc=s_v)
                                w.append((s_v, s_v.n))
                            bias = (btabb[:, (4 * i + aux):(4 * i + aux) + 1]
                                    if kind == "oth" else cbias[:, :])
                            E("scalar", lambda e, bank=bank, lo=lo, pt=ptile,
                              bias=bias: e.activation(
                                PT[:, pt * 512 + lo:pt * 512 + 512],
                                ps[bank][:, lo:512], AF.Exp, bias=bias,
                                scale=SCALE_QK), waits=w, inc=s_s,
                              rb=PT[:, ptile * 512 + 511:ptile * 512 + 512])
                            exp_wm[bi] = s_s.n

                        def emit_av(bi):
                            kind, aux, kcol = blocks[bi]
                            ptile = bi % 4
                            lo = 128 * aux if kind == "diag" else 0
                            ktile = kcol // 128
                            last = bi == nb - 1
                            E("tensor", lambda e, kt=ktile, pt=ptile, lo=lo,
                              gh=ghead, first=(bi == 0), last=last: e.matmul(
                                ps[2][:, lo:512],
                                V[:, kt * 512 + gh * 128:kt * 512 + (gh + 1) * 128],
                                PT[:, pt * 512 + lo:pt * 512 + 512],
                                start=first, stop=last),
                                waits=[(s_s, exp_wm[bi])])
                            E("tensor", lambda e, pt=ptile, lo=lo,
                              first=(bi == 0), last=last: e.matmul(
                                ps[3][0:1, lo:512], ones1b[:, :],
                                PT[:, pt * 512 + lo:pt * 512 + 512],
                                start=first, stop=last),
                                inc=s_t if last else None)

                        for bi in range(nb + 2):
                            if bi < nb:
                                emit_qk(bi)
                                emit_exp(bi)
                            if bi >= 2:
                                emit_av(bi - 2)
                        E("scalar", lambda e: e.copy(denr[:, :], ps[3][0:1, :]),
                          waits=[s_t], inc=s_s, rb=denr[0:1, 511:512])
                        E("tensor", lambda e: e.matmul(
                            ps[4][:, :], ones1f[:, :], denr[:, :], start=True,
                            stop=True), waits=[s_s, s_v], inc=s_t)
                        E("vector", lambda e: e.reciprocal(recb[:, :], ps[4][:, :]),
                          waits=[s_t], inc=s_v)
                        E("vector", lambda e, qo=qo_off: e.tensor_tensor(
                            QO[:, qo:qo + 512], ps[2][:, :], recb[:, :], ALU.mult),
                            inc=s_v)
                    flush()
                if t_ot is not None and g2 == 0:
                    dma("gpsimd", t_ot, QO[:, :], s_sp, waits=[s_v])
                    flush()

                if PHASE_LIMIT <= 5:
                    continue
                soft_barrier()
                with ExitStack() as pf:
                    scrB = sb(f"scrBf{g2}", [128, 1024], F32, pf)
                    outb4 = sb(f"outb4{g2}", [128, 4 * D], F32, pf)
                    dltw = sb(f"dltw{g2}", [128, D], F32, pf)
                    xrb = sb(f"xrb{g2}", [128, 2 * D], F32, pf)
                    f_wm = {}
                    fx_slot_wm = [0, 0]
                    for il in range(2):
                        i = 2 * g2 + il
                        for o in range(16):
                            wbuf = wqb if o % 2 == 0 else wqb2
                            w = [(s_t, f_wm[o - 2])] if o >= 2 else (
                                [(s_t, f_wm[14 + o])] if il == 1 else [s_t])
                            dma("gpsimd", wbuf[:, :].rearrange("p (c f) -> p c f", c=DCH),
                                wpd[o].rearrange("(c p) f -> p c f", p=128), s_ld,
                                waits=w)
                            bank = o % 2
                            for h in range(QH):
                                E("tensor", lambda e, h=h, o=o, il=il, bank=bank,
                                  wbuf=wbuf: e.matmul(
                                    ps[bank][:, :], wbuf[:, h * 128:(h + 1) * 128],
                                    QO[:, (h * 2 + il) * 512:(h * 2 + il + 1) * 512],
                                    start=(h == 0), stop=(h == QH - 1)),
                                    waits=[s_ld, s_s, s_v] if h == 0 else [],
                                    inc=s_t if h == QH - 1 else None)
                            f_wm[o] = s_t.n
                            E("scalar", lambda e, bank=bank: e.copy(
                                scrB[:, 0:512], ps[bank][:, :]), waits=[s_t], inc=s_s,
                                rb=scrB[:, 511:512])
                            for sub in range(NTT):
                                pb = 2 + sub % 2
                                E("tensor", lambda e, sub=sub, pb=pb: e.transpose(
                                    ps[pb][:, 0:128], scrB[:, sub * 128:(sub + 1) * 128],
                                    idn[:, :]), waits=[s_s, s_v], inc=s_t)
                                E("vector", lambda e, sub=sub, o=o, pb=pb: e.tensor_copy(
                                    outb4[:, sub * D + o * 128:sub * D + (o + 1) * 128],
                                    ps[pb][:, 0:128]), waits=[s_t, s_out], inc=s_v)
                        for sub in range(NTT):
                            ti = i * NTT + sub
                            xr = xrb[:, (sub % 2) * D:(sub % 2 + 1) * D]
                            dma("gpsimd", xr, xk[ti * 128:(ti + 1) * 128, :],
                                s_ld, waits=[s_v] if il == 0 and sub < 2
                                else [(s_v, fx_slot_wm[sub % 2])])
                            ob = outb4[:, sub * D:(sub + 1) * D]
                            E("vector", lambda e, ti=ti: e.tensor_scalar_add(
                                st8[:, 0:1], stats[:, ti * 8 + 1:ti * 8 + 2], 1.0))
                            E("vector", lambda e, ti=ti: e.tensor_tensor(
                                dltw[:, :].rearrange("p (t j) -> p t j", j=32),
                                delta[:, ti * 32:(ti + 1) * 32]
                                .unsqueeze(1).broadcast_to([128, 64, 32]),
                                oneh[:, ti * 64:(ti + 1) * 64]
                                .unsqueeze(2).broadcast_to([128, 64, 32]),
                                ALU.mult))
                            E("vector", lambda e, xr=xr, ob=ob: e.scalar_tensor_tensor(
                                scrA[:, 0:D], xr, st8[:, 0:1], ob, ALU.mult, ALU.add),
                                waits=[s_ld])
                            E("vector", lambda e, ti=ti, ob=ob: e.scalar_tensor_tensor(
                                ob, dltw[:, :], stats[:, ti * 8 + 7:ti * 8 + 8],
                                scrA[:, 0:D], ALU.add, ALU.add), inc=s_v)
                            fx_slot_wm[sub % 2] = s_v.n
                            dma("gpsimd", y[ti * 128:(ti + 1) * 128, :], ob, s_out,
                                waits=[s_v])
                        flush()

        def fin(e):
            return e.nop()
        E("gpsimd", fin, waits=[s_out])
        flush()

    return nc, taps


_PROG_CACHE = {}


def _host_tables():
    half = HD // 2
    j = np.arange(half, dtype=np.float64)
    invf = 1.0 / (10000.0 ** (j / half))
    pos = np.arange(T, dtype=np.float64)
    ang = pos[None, :] * invf[:, None]
    cos = np.cos(ang)
    sin = np.sin(ang)
    cosk = np.concatenate([cos, cos], axis=0).astype(np.float32)
    sink = np.concatenate([-sin, sin], axis=0).astype(np.float32)
    tri = np.where(np.arange(128)[None, :] >= np.arange(128)[:, None],
                   0.0, KILL).astype(np.float32)
    ident = np.eye(128, dtype=np.float32)
    return cosk, sink, tri, ident


def kernel(L0_attn_x, L0_attn_Wq, L0_attn_Wk, L0_attn_Wv, L0_attn_Wp,
           L0_ffn_ln_gamma, L0_ffn_ln_beta, L0_ffn_spline_bases,
           L0_ffn_spline_slopes, L0_ffn_output_scale):
    x = np.ascontiguousarray(np.asarray(L0_attn_x, dtype=np.float32))
    Wq = np.asarray(L0_attn_Wq, dtype=np.float32)
    Wk = np.asarray(L0_attn_Wk, dtype=np.float32)
    Wv = np.asarray(L0_attn_Wv, dtype=np.float32)
    Wp = np.asarray(L0_attn_Wp, dtype=np.float32)

    wqc = np.ascontiguousarray(
        Wq.T.reshape(D, 16, 128).transpose(1, 0, 2))
    wpc = np.ascontiguousarray(
        Wp.T.reshape(D, 16, 128).transpose(1, 0, 2))
    wkc = np.ascontiguousarray(Wk.T)
    wvc = np.ascontiguousarray(Wv.T)
    basesc = np.ascontiguousarray(
        np.asarray(L0_ffn_spline_bases, dtype=np.float32).reshape(64, 512))
    slopesc = np.ascontiguousarray(
        np.asarray(L0_ffn_spline_slopes, dtype=np.float32).reshape(64, 512))
    gammac = np.ascontiguousarray(np.asarray(L0_ffn_ln_gamma, dtype=np.float32))
    betac = np.ascontiguousarray(np.asarray(L0_ffn_ln_beta, dtype=np.float32))
    oscalec = np.ascontiguousarray(np.asarray(L0_ffn_output_scale, dtype=np.float32))

    cosk, sink, tri, ident = _host_tables()

    key = "prog"
    if key not in _PROG_CACHE:
        _PROG_CACHE[key] = build_program()
    nc, taps = _PROG_CACHE[key]

    in_maps = []
    perms = []
    for c in range(8):
        b = c // 2
        par = c % 2
        own = OWN_SLOTS[par]
        other = [s for s in range(N_ST) if s not in own]
        order = own + other
        tokperm = np.concatenate(
            [np.arange(s * ST, (s + 1) * ST) for s in order])
        perms.append(tokperm)
        xkc = np.ascontiguousarray(x[b][tokperm])
        coskc = np.ascontiguousarray(cosk[:, tokperm])
        sinkc = np.ascontiguousarray(sink[:, tokperm])
        bt = np.full((4, 4), KILL, dtype=np.float32)
        for i in range(4):
            for m in range(4):
                if m < len(other) and other[m] < own[i]:
                    bt[i, m] = -BSHIFT
        btc = np.ascontiguousarray(
            np.broadcast_to(bt.reshape(1, 16), (128, 16)).astype(np.float32))
        in_maps.append({
            "xk": xkc, "wq": wqc, "wk": wkc, "wv": wvc, "wp": wpc,
            "gamma": gammac, "beta": betac, "bases": basesc,
            "slopes": slopesc, "oscale": oscalec,
            "cosk": coskc, "sink": sinkc, "trimask": tri, "ident": ident,
            "btab": btc,
        })

    res = run_bass_kernel_spmd(nc, in_maps, core_ids=list(range(8)))

    out = np.empty((B, T, D), dtype=np.float32)
    for c in range(8):
        b = c // 2
        yv = res.results[c]["y"]
        out[b].reshape(T, D)[perms[c][:OWN_ST * ST]] = yv
    if TAPS:
        kernel.tap_results = [
            {k: res.results[c].get("tap_" + k) for k in TAPS} for c in range(8)]
    return out
